# version 03
# baseline (speedup 1.0000x reference)
import numpy as np
import ml_dtypes
import concourse.bass as bass
import concourse.bass_isa as bass_isa
import concourse.mybir as mybir
from concourse.bass_utils import run_bass_kernel_spmd
from concourse.library_config import attn as attn_lib

F32 = mybir.dt.float32
BF16 = mybir.dt.bfloat16
AF = mybir.ActivationFunctionType
ALU = mybir.AluOpType
RED = bass_isa.ReduceOp
AXX = mybir.AxisListType.X

P = 128
T = 2560
OWN0, OWN1 = 256, 2304
NOWN = OWN1 - OWN0
D = 1024
KT = 8
NCH = T // 512
MAGIC = 12582912.0
CSHIFT = 40.0
LNEPS = 1e-5

_TAPS = []
_TRUNC = [False]
_STOP = [99]
_PARANOID = [False]
_PSCOPE = [0, 0]
_PH = {}
_RACEDET = [False]
_PSRANGES = []
_OPSCACHE = {}


def _register_ops():
    if _OPSCACHE:
        return _OPSCACHE
    from concourse.dve_spec import (Spec, Src0, Src1, C0, C1, C2, Zero, One,
                                    maxx, minn, lower, Latch)
    from concourse.dve_op import DveOp
    from concourse import dve_ops

    defs = []
    _r = ((Src0 * Src1) + C0) - C0
    _lo = maxx(_r, C1)
    _hi = Zero - maxx(Zero - _lo, Latch(C1 + One))
    defs.append(("ZZACTQ", Spec(body=_hi),
                 lambda in0, in1, s0, s1, imm2: np.minimum(np.maximum(np.round(in0 * in1), s1), -(s1 + 1.0))))
    _r1 = ((Src0 * C0) + C1) - C1
    _b1 = minn(maxx(_r1, Zero - Latch(C2 + One)), C2)
    defs.append(("ZZACTQ1", Spec(body=_b1),
                 lambda in0, in1, s0, s1, imm2: np.minimum(np.maximum(np.round(in0 * s0), -(imm2 + 1.0)), imm2)))
    _r2 = ((Src0 * C0) + C1) - C1
    defs.append(("ZZWQNT", Spec(body=minn(maxx(_r2, Latch(Zero - One)), One)),
                 lambda in0, in1, s0, s1, imm2: np.minimum(np.maximum(np.round(in0 * s0), -1.0), 1.0)))
    _t2 = Src0 * Src1
    _f = (minn(maxx(_t2, Latch(Zero - C0)), C0) + C1) - C1
    defs.append(("ZZFP4A", Spec(body=_f),
                 lambda in0, in1, s0, s1, imm2: np.round(np.clip(in0 * in1, -s0, s0))))
    _b = (Src0 * Src1) - C0
    _c10 = _b > C0
    defs.append(("ZZFP4B", Spec(body=((_b > Zero) + (_b > C1)) + (_c10 + _c10)),
                 lambda in0, in1, s0, s1, imm2: (
                     ((in0 * in1 - s0) > 0.0).astype(np.float64) + ((in0 * in1 - s0) > s1)
                     + 2.0 * ((in0 * in1 - s0) > s0))))

    n = len(dve_ops.OPS)
    for i, (name, spec, reffn) in enumerate(defs):
        idx = n - len(defs) + i
        shas = {v: lower(spec, ver=v, name=name).sha(v) for v in ("v3", "v4")}
        op = DveOp(name, spec, subdim=False, uops_sha=shas, reference=reffn)
        old = dve_ops.OPS[idx]
        dve_ops.OPS[idx] = op
        dve_ops.CUSTOM_DVE_SPECS.pop(old.name, None)
        dve_ops.CUSTOM_DVE_REFS.pop(old.name, None)
        dve_ops._SUB_OPCODE_FOR_NAME.pop(old.name, None)
        dve_ops.CUSTOM_DVE_SPECS[name] = spec
        dve_ops.CUSTOM_DVE_REFS[name] = reffn
        dve_ops._SUB_OPCODE_FOR_NAME[name] = dve_ops._CUSTOM_DVE_ROW_BASE + idx
        _OPSCACHE[name] = op
    return _OPSCACHE


ENGS = ("sync", "gpsimd", "gpdma", "vector", "scalar", "scdma", "tensor")
QUEUE_OF = {"sync": "sync", "gpsimd": "gpsimd", "gpdma": "gpsimd", "vector": "vector", "scalar": "scalar", "scdma": "scalar", "tensor": "tensor"}


class Prog:
    def __init__(self):
        self.q = {en: [] for en in ("sync", "gpsimd", "vector", "scalar", "tensor")}
        self.cnt = {en: 0 for en in ENGS}
        self.hw = {en: {e2: 0 for e2 in ENGS} for en in ("sync", "gpsimd", "vector", "scalar", "tensor")}
        self.sem = {}
        self.idx = 0

    def emit(self, eng, fn, deps=(), inc=1, sw=None):
        self.idx += 1
        if _PARANOID[0] or _PSCOPE[0] <= self.idx < _PSCOPE[1] or \
           any(lo <= self.idx < hi for lo, hi in _PSRANGES):
            deps = list(deps) + [(en, self.cnt[en]) for en in ENGS]
        q = QUEUE_OF[eng]
        real = []
        for (en, c) in deps:
            if QUEUE_OF[en] == q or c <= 0:
                continue
            if c > self.hw[q][en]:
                real.append((en, c))
                self.hw[q][en] = c
        self.q[q].append((tuple(real), fn, inc, eng, sw))
        self.cnt[eng] += inc
        return self.cnt[eng]

    def now(self, eng):
        return self.cnt[eng]

    def build(self, block):
        sem = self.sem
        def mk(qname):
            items = self.q[qname]
            selfser = qname == "vector"
            def body(h):
                prev = 0
                waited = 0
                for deps, fn, inc, semname, sw in items:
                    if selfser and sw is not False:
                        tgt = prev if sw is None else min(sw, prev)
                        if tgt > waited:
                            h.wait_ge(sem[qname], tgt)
                            waited = tgt
                    for (en, c) in deps:
                        h.wait_ge(sem[en], c)
                    r = fn(h)
                    if inc:
                        r.then_inc(sem[semname], inc)
                    if semname == qname:
                        prev += inc
            return body
        for name, fn in (("sync", block.sync), ("gpsimd", block.gpsimd), ("vector", block.vector),
                         ("scalar", block.scalar), ("tensor", block.tensor)):
            if self.q[name]:
                fn(mk(name))


def build_nc(taps=()):
    ops = _register_ops()
    nc = bass.Bass("TRN2", target_bir_lowering=False, detect_race_conditions=_RACEDET[0])

    def din(name, shape, dt=F32):
        return nc.dram_tensor(name, list(shape), dt, kind="ExternalInput").ap()

    xT = din("xT", (D, T))
    kvmask = din("kvmask", (P, T), BF16)
    identb = din("identb", (P, P), BF16)
    cmask = din("cmask", (P, 256))
    smal = din("smal", (P, 160))
    strps = din("strps", (1, 3 * 4096))
    w0qvT = din("w0qvT", (D, 2048))
    w0oT = din("w0oT", (D, D))
    wmod0 = din("wmod0", (D, 6144))
    w1T = din("w1T", (D, 4096))
    w2T = din("w2T", (4096, D))
    w1qkvT = din("w1qkvT", (D, 3072))
    w1oT = din("w1oT", (D, D))
    wgO = din("wgO", (2816, D))
    wuO = din("wuO", (2816, D))
    wdO = din("wdO", (D, 2816))
    wmod1 = din("wmod1", (D, 6144))
    outT = nc.dram_tensor("outT", [D, NOWN], F32, kind="ExternalOutput").ap()
    tapout = {t: nc.dram_tensor("tap_" + t, [D, T], F32, kind="ExternalOutput").ap() for t in taps}

    sc_w1 = nc.dram_tensor("sc_w1", [D, 4096], BF16, kind="Internal").ap()
    sc_w2 = nc.dram_tensor("sc_w2", [4096, D], BF16, kind="Internal").ap()
    sc_wg = nc.dram_tensor("sc_wg", [D, 2816], BF16, kind="Internal").ap()
    sc_wu = nc.dram_tensor("sc_wu", [D, 2816], BF16, kind="Internal").ap()
    sc_wd = nc.dram_tensor("sc_wd", [2816, D], BF16, kind="Internal").ap()
    sc_qkv = nc.dram_tensor("sc_qkv", [D, 3072], BF16, kind="Internal").ap()
    sc_o = nc.dram_tensor("sc_o", [D, D], BF16, kind="Internal").ap()

    SM = {}
    _c = [0]
    for nm, n in [("c0T", 8), ("c1T", 8), ("b0qv", 16), ("b0o", 8), ("wqP", 24), ("wvP", 24),
                  ("ln1g", 8), ("ln1b", 8), ("b2", 8), ("b1qkv", 24), ("b1o", 8),
                  ("negc", 1), ("eps", 1)]:
        SM[nm] = _c[0]; _c[0] += n
    assert _c[0] <= 160

    from contextlib import ExitStack
    with ExitStack() as ctx:
        e = ctx.enter_context
        xres = e(nc.sbuf_tensor("xres", [P, KT * T], F32))
        pool = e(nc.sbuf_tensor("pool", [P, 52600], BF16))
        poolf = pool.ap().bitcast(F32)
        wba = e(nc.sbuf_tensor("wba", [P, 1024], BF16))
        wbb = e(nc.sbuf_tensor("wbb", [P, 1024], BF16))
        sm = e(nc.sbuf_tensor("sm", [P, 160], F32))
        cmk = e(nc.sbuf_tensor("cmk", [P, 256], F32))
        idb = e(nc.sbuf_tensor("idb", [P, P], BF16))
        kvm = e(nc.sbuf_tensor("kvm", [P, T], BF16))
        strip = e(nc.sbuf_tensor("strip", [1, 1024], F32))
        modP = e(nc.sbuf_tensor("modP", [P, 96], F32))
        svec = e(nc.sbuf_tensor("svec", [P, 96], F32))
        onesf = e(nc.sbuf_tensor("onesf", [P, 1], F32))
        onesb = e(nc.sbuf_tensor("onesb", [P, 1], BF16))
        murep = e(nc.sbuf_tensor("murep", [P, 512], F32))
        ivrep = e(nc.sbuf_tensor("ivrep", [P, 512], F32))
        srep = e(nc.sbuf_tensor("srep", [P, 512], F32))
        psA = e(nc.psum_tensor("psA", [P, 512], F32))
        psB = e(nc.psum_tensor("psB", [P, 512], F32))
        psC = e(nc.psum_tensor("psC", [P, 512], F32))
        psSC = e(nc.psum_tensor("psSC", [P, 1024], F32))
        psO = e(nc.psum_tensor("psO", [65, 1024], F32))
        psD = e(nc.psum_tensor("psD", [P, 512], F32))
        psTb = psD.ap().bitcast(BF16)
        psSCb = psSC.ap().bitcast(BF16)
        sems = {en: e(nc.semaphore("s_" + en)) for en in ENGS}
        block = e(nc.Block())

        p = Prog()
        p.sem = sems

        def dma(eng, out, in_, deps=()):
            sname = {"gpsimd": "gpdma", "scalar": "scdma"}.get(eng, eng)
            return p.emit(sname, lambda h: h.dma_start(out=out, in_=in_), deps, inc=16)
        V = lambda fn, deps=(), sw=None: p.emit("vector", fn, deps, 1, sw)
        S = lambda fn, deps=(): p.emit("scalar", fn, deps, 1)
        G = lambda fn, deps=(): p.emit("gpsimd", fn, deps, 1)
        TT = lambda fn, deps=(): p.emit("tensor", fn, deps, 1)

        def psdep(*extra):
            return [("vector", p.now("vector")), ("scalar", p.now("scalar"))] + list(extra)

        bank_r = {}
        def breg(bank, eng="vector", cnt=None):
            d = bank_r.setdefault(bank, {})
            c = p.now(eng) if cnt is None else cnt
            if c > d.get(eng, 0):
                d[eng] = c
        def bdep(*banks):
            dd = {}
            for b in banks:
                for en, c in bank_r.get(b, {}).items():
                    if c > dd.get(en, 0):
                        dd[en] = c
            return list(dd.items())

        def smc(nm, i0=0, n=1):
            c = SM[nm]
            return sm.ap()[:, c + i0:c + i0 + n]

        def pb(off, n):
            assert off + n <= 52600, (off, n)
            return pool.ap()[:, off:off + n]

        def pf(off, n):
            assert off % 2 == 0 and off + 2 * n <= 52600, (off, n)
            return poolf[:, off // 2:off // 2 + n]

        xr = xres.ap()
        def xslice(kt, c0, n):
            return xr[:, kt * T + c0: kt * T + c0 + n]


        def finish_now():
            c_fin0 = p.now("vector")
            for kt in range(KT):
                dma("sync", outT.rearrange("(k p) n -> p k n", p=P)[:, kt:kt + 1, :],
                    xr[:, kt * T + OWN0:kt * T + OWN1].rearrange("p (o n) -> p o n", o=1),
                    deps=[("vector", c_fin0), ("gpdma", p.now("gpdma")), ("scalar", p.now("scalar")),
                          ("scdma", p.now("scdma")), ("tensor", p.now("tensor")), ("gpsimd", p.now("gpsimd"))])
            p.emit("gpsimd", lambda h: h.engine_nop(), deps=[("sync", p.now("sync"))], inc=0)
            p.build(block)
            import concourse.mybir as _mb
            _mb.codegen_inst_isa_subclasses(nc)
            return nc
        dma("sync", sm.ap()[:], smal)
        dma("sync", cmk.ap()[:], cmask)
        dma("sync", idb.ap()[:], identb)
        dma("sync", kvm.ap()[:], kvmask)
        c_smalls = p.now("sync")
        for k in range(4):
            dma("sync", xr[:, 2 * k * T:(2 * k + 2) * T].rearrange("p (k n) -> p k n", k=2),
                xT.rearrange("(k p) n -> p k n", p=P)[:, 2 * k:2 * k + 2, :])
        c_xload = p.now("sync")
        p.emit("gpsimd", lambda h: h.load_library(attn_lib), deps=(), inc=0)
        c_lib = G(lambda h: h.engine_nop())
        V(lambda h: h.memset(onesf.ap()[:], 1.0))
        c_ones = V(lambda h: h.memset(onesb.ap()[:], 1.0))

        for j, ws in enumerate([2.0 ** -4.5, 2.0 ** -4.5, 2.0 ** -3.0]):
            V(lambda h, j=j: h.tensor_mul(svec.ap()[:, 64 + 8 * j:72 + 8 * j],
                                          smc("wqP", 8 * j, 8), smc("wvP", 8 * j, 8)),
              deps=[("sync", c_smalls)])
            V(lambda h, j=j, ws=ws: h.tensor_scalar_mul(svec.ap()[:, 64 + 8 * j:72 + 8 * j],
                                                        svec.ap()[:, 64 + 8 * j:72 + 8 * j], ws))
        c_wc = p.now("vector")

        _trs = {"i": 0}
        _TRB = ("D", "SC0", "SC1")
        def transpose2(in_ap, np_, nf, dst_ap, deps=()):
            i = _trs["i"] % 3
            _trs["i"] += 1
            sl = [psTb[0:nf, 0:np_], psSCb[0:nf, 0:np_], psSCb[0:nf, 1024:1024 + np_]][i]
            bp = in_ap.base_partition()
            ct = TT(lambda h: h.transpose(sl, in_ap, idb.ap()[bp:bp + np_, bp:bp + np_]),
                    deps=list(deps) + bdep(_TRB[i]))
            cv = V(lambda h: h.tensor_copy(dst_ap, sl), deps=[("tensor", ct)], sw=False)
            breg(_TRB[i], "vector", cv)
            return cv

        def strip_transpose(src_1xN, dst_col, deps=()):
            i = _trs["i"] % 3
            _trs["i"] += 1
            sl = [psD.ap()[:, 0:1], psSC.ap()[:, 0:1], psSC.ap()[:, 512:513]][i]
            ct = TT(lambda h: h.transpose(sl, src_1xN, onesf.ap()[0:1, 0:1]),
                    deps=list(deps) + bdep(_TRB[i]))
            cv = V(lambda h: h.tensor_copy(dst_col, sl), deps=[("tensor", ct)], sw=False)
            breg(_TRB[i], "vector", cv)
            return cv

        def wprep_bit(wsrc, rows, cols, sc_dst, svcol):
            kt = rows // P
            cw = 4096 // kt
            nchunks = cols // cw
            csrc = wsrc.rearrange("(k p) n -> p k n", p=P)
            cdst = sc_dst.rearrange("(k p) n -> p k n", p=P)
            BUF = [0, 4096]
            ABSB = 8192
            acc = svec.ap()[:, 8:8 + nchunks]
            for j in range(nchunks):
                b0 = BUF[j % 2]
                dep = ([("scalar", p.now("scalar"))] if j >= 2 else
                       [("gpsimd", c_lib), ("vector", p.now("vector")), ("sync", p.now("sync"))])
                cdma = dma("gpsimd", pb(b0, 4096).rearrange("p (k c) -> p k c", k=kt),
                           csrc[:, :, j * cw:(j + 1) * cw], deps=dep)
                sdep = [("gpdma", cdma)] + ([("sync", p.now("sync"))] if j < 2 else [])
                S(lambda h, b0=b0, j=j: h.activation(pb(ABSB, 4096), pb(b0, 4096),
                                                     AF.Abs, accum_out=acc[:, j:j + 1]),
                  deps=sdep)
            cr = V(lambda h, n=nchunks: h.reduce_sum(svec.ap()[:, 20:21], acc[:, 0:n], axis=AXX),
                   deps=[("scalar", p.now("scalar"))])
            cg = G(lambda h: h.partition_all_reduce(svec.ap()[:, 21:22], svec.ap()[:, 20:21], P, RED.add),
                   deps=[("vector", cr)])
            V(lambda h: h.tensor_scalar(svec.ap()[:, 22:23], svec.ap()[:, 21:22],
                                        1.0 / (rows * cols), 1e-5, ALU.mult, ALU.max),
              deps=[("gpsimd", cg)])
            cv = V(lambda h: h.reciprocal(svec.ap()[:, svcol:svcol + 1], svec.ap()[:, 22:23]))
            for j in range(nchunks):
                b0 = BUF[j % 2]
                dep = [("sync", p.now("sync"))] if j >= 2 else [("vector", cv)]
                cdma = dma("gpsimd", pb(b0, 4096).rearrange("p (k c) -> p k c", k=kt),
                           csrc[:, :, j * cw:(j + 1) * cw], deps=dep)
                cq = V(lambda h, b0=b0: h._custom_dve(ops["ZZWQNT"], out=pb(ABSB, 4096), in0=pb(b0, 4096),
                                                      s0=svec.ap()[:, svcol:svcol + 1], s1=MAGIC),
                       deps=[("gpdma", cdma), ("sync", p.now("sync"))])
                dma("sync", cdst[:, :, j * cw:(j + 1) * cw],
                    pb(ABSB, 4096).rearrange("p (k c) -> p k c", k=kt), deps=[("vector", cq)])
            return p.now("sync")

        _PH[0] = p.idx
        if _STOP[0] <= 0:
            return finish_now()
        wprep_bit(w1T, D, 4096, sc_w1, 0)
        wprep_bit(w2T, 4096, D, sc_w2, 1)
        V(lambda h: h.reciprocal(svec.ap()[:, 2:3], svec.ap()[:, 0:1]))
        c_wbit = V(lambda h: h.reciprocal(svec.ap()[:, 3:4], svec.ap()[:, 1:2]))

        def wcast(wsrc, dst, ncols):
            csrc = wsrc.rearrange("(k p) n -> p k n", p=P)
            cdst = dst.rearrange("(k p) n -> p k n", p=P)
            for j in range(ncols // 512):
                b0 = (j % 2) * 4096
                cld = dma("gpsimd", pb(b0, 4096).rearrange("p (k c) -> p k c", k=KT),
                          csrc[:, :, j * 512:(j + 1) * 512],
                          deps=[("vector", p.now("vector")), ("sync", p.now("sync")),
                                ("gpdma", p.now("gpdma"))])
                dma("sync", cdst[:, :, j * 512:(j + 1) * 512],
                    pb(b0, 4096).rearrange("p (k c) -> p k c", k=KT), deps=[("gpdma", cld)])
        wcast(w1qkvT, sc_qkv, 3072)
        wcast(w1oT, sc_o, 1024)
        c_castdone = p.now("sync")

        def fp4_qdq(src_off, n, nb, f1_off, u_off, abs_off, dst_off, S_off, deps=()):
            c1 = V(lambda h: h.reduce_max(pf(S_off, nb), pb(src_off, n).rearrange("p (b s) -> p b s", s=16),
                                          axis=AXX, apply_absolute_value=True), deps=list(deps))
            cab = S(lambda h: h.activation(pb(abs_off, n), pb(src_off, n), AF.Abs), deps=list(deps))
            V(lambda h: h.tensor_scalar(pf(S_off + 2 * nb, nb), pf(S_off, nb), 1.0 / 12, 5e-13, ALU.mult, ALU.max))
            V(lambda h: h.reciprocal(pf(S_off + 2 * nb, nb), pf(S_off + 2 * nb, nb)))
            V(lambda h: h.tensor_copy(pb(S_off + 4 * nb, nb), pf(S_off + 2 * nb, nb)))
            V(lambda h: h.tensor_scalar(pf(S_off + 2 * nb, nb), pf(S_off, nb), 1.0 / 24, 2.5e-13, ALU.mult, ALU.max))
            V(lambda h: h.tensor_copy(pb(S_off + 5 * nb, nb), pf(S_off + 2 * nb, nb)))
            r2 = pb(S_off + 4 * nb, nb).rearrange("p (b o) -> p b o", o=1).broadcast_to([P, nb, 16])
            sq = pb(S_off + 5 * nb, nb).rearrange("p (b o) -> p b o", o=1).broadcast_to([P, nb, 16])
            s3 = pb(src_off, n).rearrange("p (b s) -> p b s", s=16)
            a3 = pb(abs_off, n).rearrange("p (b s) -> p b s", s=16)
            V(lambda h: h._custom_dve(ops["ZZFP4A"], out=pb(f1_off, n).rearrange("p (b s) -> p b s", s=16),
                                      in0=s3, in1=r2, s0=4.4, s1=MAGIC))
            V(lambda h: h._custom_dve(ops["ZZFP4B"], out=pb(u_off, n).rearrange("p (b s) -> p b s", s=16),
                                      in0=a3, in1=r2, s0=5.0, s1=2.0), deps=[("scalar", cab)])
            V(lambda h: h.scalar_tensor_tensor(pb(u_off, n), pb(u_off, n), 2.0, pb(f1_off, n),
                                               ALU.add, ALU.mult))
            return V(lambda h: h.tensor_tensor(pb(dst_off, n).rearrange("p (b s) -> p b s", s=16),
                                               pb(u_off, n).rearrange("p (b s) -> p b s", s=16), sq, ALU.mult))

        def wprep_fp4b(wsrc, outdim, indim, sc_dst):
            optile = outdim // P
            ikt = indim // P
            nr = max(1, 4096 // indim)
            csrc = wsrc.rearrange("(k p) n -> p k n", p=P)
            cdst = sc_dst.rearrange("(k p) n -> p k n", p=P)
            W_, O_f1, O_u, O_tq, S_off, ABSW = 0, 4096, 8192, 12288, 16384, 18432
            r0 = 0
            while r0 < optile:
                rr = min(nr, optile - r0)
                cw = rr * indim
                nb = cw // 16
                cdma = dma("gpsimd", pb(W_, cw).rearrange("p (k c) -> p k c", k=rr),
                           csrc[:, r0:r0 + rr, :],
                           deps=[("sync", p.now("sync")), ("gpsimd", c_lib), ("vector", p.now("vector")),
                                 ("tensor", p.now("tensor"))])
                cdq = fp4_qdq(W_, cw, nb, O_f1, O_u, ABSW, W_, S_off, deps=[("gpdma", cdma)])
                csw = p.now("sync")
                for bki in range(ikt):
                    for a in range(rr):
                        transpose2(pb(W_ + a * indim + bki * P, P), P, P,
                                   pb(O_tq + bki * rr * P + a * P, P),
                                   deps=[("vector", cdq), ("sync", csw)])
                cst = p.now("vector")
                for bki in range(ikt):
                    dma("sync", cdst[:, bki:bki + 1, r0 * P:(r0 + rr) * P],
                        pb(O_tq + bki * rr * P, rr * P).rearrange("p (o c) -> p o c", o=1),
                        deps=[("vector", cst)])
                r0 += rr
            return p.now("sync")

        wprep_fp4b(wgO, 2816, D, sc_wg)
        wprep_fp4b(wuO, 2816, D, sc_wu)
        wprep_fp4b(wdO, D, 2816, sc_wd)

        def do_mod(wmod, ccol, L):
            csil = S(lambda h: h.activation(svec.ap()[:, 24:32], smc(ccol, 0, 8), AF.Silu),
                     deps=[("sync", c_smalls)])
            ACC = 24576
            cz = V(lambda h: h.memset(pf(ACC, 6144)[0:1, :], 0.0),
                   deps=[("tensor", p.now("tensor")), ("sync", p.now("sync"))])
            for t in range(KT):
                off = (t % 2) * 12288
                cd = dma("sync", pf(off, 6144), wmod.rearrange("(k p) n -> p k n", p=P)[:, t:t + 1, :],
                         deps=[("tensor", p.now("tensor")) if t >= 2 else ("vector", cz)])
                for nchk in range(12):
                    bnk = "A" if nchk % 2 == 0 else "B"
                    sl = (psA if nchk % 2 == 0 else psB).ap()[0:1, :]
                    cm = TT(lambda h, off=off, nchk=nchk, sl=sl, t=t: h.matmul(
                        sl, svec.ap()[:, 24 + t:25 + t], pf(off, 6144)[:, nchk * 512:(nchk + 1) * 512],
                        start=True, stop=True, skip_group_check=True),
                        deps=bdep(bnk) + [("sync", cd), ("scalar", csil)])
                    V(lambda h, nchk=nchk, sl=sl: h.tensor_add(
                        pf(ACC, 6144)[0:1, nchk * 512:(nchk + 1) * 512],
                        pf(ACC, 6144)[0:1, nchk * 512:(nchk + 1) * 512], sl),
                      deps=[("tensor", cm)])
                    breg(bnk)
            cacc = p.now("vector")
            for seg in range(48):
                strip_transpose(pf(ACC, 6144)[0:1, seg * P:(seg + 1) * P],
                                modP.ap()[:, L * 48 + seg:L * 48 + seg + 1],
                                deps=[("vector", cacc)])
            base = L * 48
            for v in (2, 5):
                cs = base + v * 8
                c1 = V(lambda h, cs=cs: h.tensor_scalar(svec.ap()[:, 40:48], modP.ap()[:, cs:cs + 8],
                                                        0.0, 2.0, ALU.is_ge, ALU.mult))
                V(lambda h: h.tensor_scalar_add(svec.ap()[:, 40:48], svec.ap()[:, 40:48], -1.0))
                c3 = S(lambda h, cs=cs: h.activation(svec.ap()[:, 48:56], modP.ap()[:, cs:cs + 8], AF.Abs),
                       deps=[("vector", c1)])
                V(lambda h: h.tensor_scalar_max(svec.ap()[:, 48:56], svec.ap()[:, 48:56], 0.25),
                  deps=[("scalar", c3)])
                V(lambda h, cs=cs: h.tensor_mul(modP.ap()[:, cs:cs + 8], svec.ap()[:, 40:48], svec.ap()[:, 48:56]))
            for v in (1, 4):
                cs = base + v * 8
                V(lambda h, cs=cs: h.tensor_scalar_add(modP.ap()[:, cs:cs + 8], modP.ap()[:, cs:cs + 8], 1.0))
            return p.now("vector")

        _PH[1] = p.idx
        if _STOP[0] <= 1:
            return finish_now()
        c_mod0 = do_mod(wmod0, "c0T", 0)

        def ln_chunk(col0, w, hoff, cvcol, svcol, sq_off, extra=()):
            V(lambda h: h.memset(svec.ap()[:, 95:96], 0.0),
              deps=list(extra) + [("tensor", p.now("tensor"))])
            for kt in range(KT):
                off = sq_off + (kt % 2) * 2 * w
                csq = S(lambda h, kt=kt, off=off: h.activation(pf(off, w), xslice(kt, col0, w), AF.Square),
                        deps=list(extra) + [("tensor", p.now("tensor")), ("vector", p.now("vector"))])
                TT(lambda h, kt=kt: h.matmul(psA.ap()[0:1, 0:w], onesf.ap()[:], xslice(kt, col0, w),
                                             start=(kt == 0), stop=(kt == KT - 1), skip_group_check=True),
                   deps=(bdep("A") + [("vector", p.now("vector"))] + list(extra)) if kt == 0 else [("vector", c_ones)])
                TT(lambda h, kt=kt, off=off: h.matmul(psB.ap()[0:1, 0:w], onesf.ap()[:], pf(off, w),
                                                      start=(kt == 0), stop=(kt == KT - 1), skip_group_check=True),
                   deps=([("scalar", csq)] + bdep("B")) if kt == 0 else [("scalar", csq)])
            cmm = p.now("tensor")
            V(lambda h: h.tensor_scalar_mul(murep.ap()[0:1, 0:w], psA.ap()[0:1, 0:w], 1.0 / D),
              deps=[("tensor", cmm)])
            breg("A")
            V(lambda h: h.tensor_scalar_mul(ivrep.ap()[0:1, 0:w], psB.ap()[0:1, 0:w], 1.0 / D))
            breg("B")
            V(lambda h: h.tensor_mul(srep.ap()[0:1, 0:w], murep.ap()[0:1, 0:w], murep.ap()[0:1, 0:w]))
            V(lambda h: h.tensor_sub(ivrep.ap()[0:1, 0:w], ivrep.ap()[0:1, 0:w], srep.ap()[0:1, 0:w]))
            cs = S(lambda h: h.activation(ivrep.ap()[0:1, 0:w], ivrep.ap()[0:1, 0:w], AF.Sqrt,
                                          bias=smc("eps")[0:1, :]), deps=[("vector", p.now("vector"))])
            cv = V(lambda h: h.reciprocal(ivrep.ap()[0:1, 0:w], ivrep.ap()[0:1, 0:w]), deps=[("scalar", cs)])
            cb1 = G(lambda h: h.partition_broadcast(murep.ap()[:, 0:w], murep.ap()[0:1, 0:w], P),
                    deps=[("vector", cv), ("gpsimd", c_lib)])
            cb2 = G(lambda h: h.partition_broadcast(ivrep.ap()[:, 0:w], ivrep.ap()[0:1, 0:w], P),
                    deps=[("vector", cv)])
            out_c = 0
            for kt in range(KT):
                off = sq_off + (kt % 2) * 2 * w
                V(lambda h, kt=kt, off=off: h.tensor_sub(pf(off, w), xslice(kt, col0, w), murep.ap()[:, 0:w]),
                  deps=[("gpsimd", cb2)])
                V(lambda h, off=off: h.tensor_mul(pf(off, w), pf(off, w), ivrep.ap()[:, 0:w]))
                out_c = V(lambda h, kt=kt, off=off: h.tensor_scalar(
                    pb(hoff + kt * w, w), pf(off, w),
                    modP.ap()[:, cvcol + kt:cvcol + kt + 1], modP.ap()[:, svcol + kt:svcol + kt + 1],
                    ALU.mult, ALU.add))
            return out_c

        WQV, W0O = 0, 16384
        H1, A0 = 24576, 28672
        QF, VF, W1Q, W1V, W2Q, W2V, SQX = 32768, 34816, 36864, 38912, 40960, 42496, 44544
        cwqv = dma("gpsimd", pb(WQV, KT * 2048).rearrange("p (k c) -> p k c", k=KT),
                   w0qvT.rearrange("(k p) n -> p k n", p=P),
                   deps=[("sync", p.now("sync")), ("vector", p.now("vector")), ("tensor", p.now("tensor"))])
        cw0o = dma("gpsimd", pb(W0O, KT * 1024).rearrange("p (k c) -> p k c", k=KT),
                   w0oT.rearrange("(k p) n -> p k n", p=P))
        for j in range(NCH):
            cc = j * 512
            chv = ln_chunk(cc, 512, H1, 0 * 48 + 1 * 8, 0 * 48 + 0 * 8, SQX,
                           extra=[("sync", c_xload), ("vector", c_mod0)])
            for pair in range(KT):
                for kt in range(KT):
                    TT(lambda h, kt=kt, pair=pair: h.matmul(
                        psA.ap()[:], pb(WQV + kt * 2048 + pair * P, P), pb(H1 + kt * 512, 512),
                        start=(kt == 0), stop=(kt == KT - 1), skip_group_check=True),
                       deps=((bdep("A") + [("vector", chv), ("gpdma", cwqv)]) if kt == 0 else []))
                cq = p.now("tensor")
                for kt in range(KT):
                    TT(lambda h, kt=kt, pair=pair: h.matmul(
                        psB.ap()[:], pb(WQV + kt * 2048 + 1024 + pair * P, P), pb(H1 + kt * 512, 512),
                        start=(kt == 0), stop=(kt == KT - 1), skip_group_check=True),
                       deps=(bdep("B") if kt == 0 else []))
                cv2 = p.now("tensor")
                cqe = V(lambda h, pair=pair: h.tensor_scalar_add(pf(QF, 512), psA.ap()[:], smc("b0qv", pair)),
                        deps=[("tensor", cq)])
                breg("A")
                cve = V(lambda h, pair=pair: h.tensor_scalar_add(pf(VF, 512), psB.ap()[:], smc("b0qv", 8 + pair)),
                        deps=[("tensor", cv2)])
                breg("B")
                for (src, W1o, W2o) in ((QF, W1Q, W2Q), (VF, W1V, W2V)):
                    V(lambda h, src=src, W1o=W1o: h.tensor_sub(
                        pf(W1o, 256), pf(src, 512)[:, 0:512:2], pf(src, 512)[:, 1:512:2]))
                    V(lambda h, src=src, W1o=W1o: h.tensor_add(
                        pf(W1o + 512, 256), pf(src, 512)[:, 0:512:2], pf(src, 512)[:, 1:512:2]))
                    V(lambda h, W1o=W1o, W2o=W2o: h.tensor_sub(
                        pf(W2o, 128), pf(W1o + 512, 256)[:, 0:256:2], pf(W1o + 512, 256)[:, 1:256:2]))
                    V(lambda h, W1o=W1o, W2o=W2o: h.tensor_add(
                        pf(W2o + 256, 128), pf(W1o + 512, 256)[:, 0:256:2], pf(W1o + 512, 256)[:, 1:256:2]))
                    V(lambda h, W2o=W2o: h.tensor_sub(
                        pf(W2o + 512, 64), pf(W2o + 256, 128)[:, 0:128:2], pf(W2o + 256, 128)[:, 1:128:2]))
                    V(lambda h, W2o=W2o: h.tensor_add(
                        pf(W2o + 640, 64), pf(W2o + 256, 128)[:, 0:128:2], pf(W2o + 256, 128)[:, 1:128:2]))
                V(lambda h, pair=pair: h.scalar_tensor_tensor(
                    pf(QF, 64), pf(W2Q + 640, 64), svec.ap()[:, 64 + pair:65 + pair], pf(W2V + 640, 64),
                    ALU.mult, ALU.mult))
                V(lambda h, pair=pair: h.scalar_tensor_tensor(
                    pf(QF + 128, 64), pf(W2Q + 512, 64), svec.ap()[:, 72 + pair:73 + pair], pf(W2V + 512, 64),
                    ALU.mult, ALU.mult))
                V(lambda h, pair=pair: h.scalar_tensor_tensor(
                    pf(QF + 256, 128), pf(W2Q, 128), svec.ap()[:, 80 + pair:81 + pair], pf(W2V, 128),
                    ALU.mult, ALU.mult))
                V(lambda h: h.scalar_tensor_tensor(
                    pf(VF, 256), pf(W1Q, 256), 2.0 ** -1.5, pf(W1V, 256), ALU.mult, ALU.mult))
                V(lambda h: h.tensor_add(pf(QF + 1024, 128)[:, 0:128:2], pf(QF, 64), pf(QF + 128, 64)))
                V(lambda h: h.tensor_sub(pf(QF + 1024, 128)[:, 1:128:2], pf(QF, 64), pf(QF + 128, 64)))
                V(lambda h: h.tensor_add(pf(W1Q, 256)[:, 0:256:2], pf(QF + 1024, 128), pf(QF + 256, 128)))
                V(lambda h: h.tensor_sub(pf(W1Q, 256)[:, 1:256:2], pf(QF + 1024, 128), pf(QF + 256, 128)))
                ab = pb(A0 + pair * 512, 512).rearrange("p (a b) -> p a b", b=2)
                V(lambda h, ab=ab: h.tensor_tensor(
                    ab[:, :, 0:1], pf(W1Q, 256).rearrange("p (a b) -> p a b", b=1),
                    pf(VF, 256).rearrange("p (a b) -> p a b", b=1), ALU.add))
                V(lambda h, ab=ab: h.tensor_tensor(
                    ab[:, :, 1:2], pf(W1Q, 256).rearrange("p (a b) -> p a b", b=1),
                    pf(VF, 256).rearrange("p (a b) -> p a b", b=1), ALU.subtract))
            c_att = p.now("vector")
            for m in range(KT):
                for kt in range(KT):
                    TT(lambda h, m=m, kt=kt: h.matmul(
                        psC.ap()[:], pb(W0O + kt * 1024 + m * P, P), pb(A0 + kt * 512, 512),
                        start=(kt == 0), stop=(kt == KT - 1), skip_group_check=True),
                       deps=((bdep("C") + [("vector", c_att), ("gpdma", cw0o)]) if kt == 0 else []))
                cmm = p.now("tensor")
                cev = V(lambda h, m=m: h.tensor_scalar(pf(SQX, 512), psC.ap()[:],
                                                 smc("b0o", m), modP.ap()[:, 2 * 8 + m:2 * 8 + m + 1],
                                                 ALU.add, ALU.mult), deps=[("tensor", cmm)], sw=False)
                breg("C")
                V(lambda h, m=m, cc=cc: h.tensor_add(xslice(m, cc, 512), xslice(m, cc, 512), pf(SQX, 512)),
                  sw=cev)

        def tap(tname):
            if tname in tapout:
                cfin = p.now("vector")
                for kt in range(KT):
                    dma("sync", tapout[tname].rearrange("(k p) n -> p k n", p=P)[:, kt:kt + 1, :],
                        xr[:, kt * T:(kt + 1) * T].rearrange("p (o n) -> p o n", o=1),
                        deps=[("vector", cfin)])
        if _STOP[0] == 15:
            cmp_ = p.now("vector")
            dma("sync", outT.rearrange("(k p) n -> p k n", p=P)[:, 0:1, 0:96],
                modP.ap()[:].rearrange("p (o n) -> p o n", o=1), deps=[("vector", cmp_)])
            p.emit("gpsimd", lambda h: h.engine_nop(), deps=[("sync", p.now("sync"))], inc=0)
            p.build(block)
            import concourse.mybir as _mb
            _mb.codegen_inst_isa_subclasses(nc)
            return nc
        _PH[2] = p.idx
        if _STOP[0] <= 2:
            return finish_now()
        tap("xattn0")

        TCH = 256
        H2, QI1, XLN = 0, 2048, 4096
        ABSA, ABSC = 6144, 7168
        WS1B = 4096
        H4A, Q2A, Q2F, SQS = 8192, 16384, 20480, 28672
        TFX = 32768
        RB1, RG2, RB2 = 34816, 38912, 43008
        WS1 = 47104
        SQX2 = 33792
        cstr = dma("sync", strip[0:1, 0:1024], strps[0:1, 0:1024],
                   deps=[("sync", p.now("sync")), ("vector", p.now("vector")), ("tensor", p.now("tensor"))])
        crep = 0
        for rj, roff in enumerate((RB1, RG2, RB2)):
            for seg in range(4):
                cs2 = V(lambda h: h.tensor_copy(pb(0, 0) if False else pool.ap()[0:1, 47104:47104 + 1024],
                                                strip[0:1, 0:1024]),
                        deps=[("sync", cstr), ("gpsimd", crep)])
                crep = G(lambda h, roff=roff, seg=seg: h.partition_broadcast(
                    pb(roff + seg * 1024, 1024), pool.ap()[0:1, 47104:47104 + 1024], P),
                    deps=[("vector", cs2)])
                if rj * 4 + seg < 11:
                    nxt = (rj * 4 + seg + 1)
                    cstr = dma("sync", strip[0:1, 0:1024], strps[0:1, nxt * 1024:(nxt + 1) * 1024],
                               deps=[("vector", p.now("vector"))])
        c_reps = crep

        wrd = {}
        for tch in range(T // TCH):
            cc = tch * TCH
            _PH[(tch, "ln")] = p.idx
            chv = ln_chunk(cc, TCH, H2, 0 * 48 + 4 * 8, 0 * 48 + 3 * 8, TFX,
                           extra=[("gpsimd", c_reps)])
            _PH[(tch, "stats")] = p.idx
            for kt in range(KT):
                off = ABSA + (kt % 2) * 512
                csq = S(lambda h, kt=kt, off=off: h.activation(pf(off, TCH), pb(H2 + kt * TCH, TCH), AF.Square),
                        deps=[("vector", chv), ("tensor", p.now("tensor"))])
                TT(lambda h, kt=kt: h.matmul(psA.ap()[0:1, 0:TCH], onesb.ap()[:], pb(H2 + kt * TCH, TCH),
                                             start=(kt == 0), stop=(kt == KT - 1), skip_group_check=True),
                   deps=((bdep("A") + [("vector", chv)]) if kt == 0 else []))
                TT(lambda h, kt=kt, off=off: h.matmul(psB.ap()[0:1, 0:TCH], onesf.ap()[:], pf(off, TCH),
                                                      start=(kt == 0), stop=(kt == KT - 1), skip_group_check=True),
                   deps=([("scalar", csq)] + bdep("B")) if kt == 0 else [("scalar", csq)])
            cmm = p.now("tensor")
            V(lambda h: h.tensor_scalar_mul(murep.ap()[0:1, 0:TCH], psA.ap()[0:1, 0:TCH], 1.0 / D),
              deps=[("tensor", cmm)])
            breg("A")
            V(lambda h: h.tensor_scalar_mul(ivrep.ap()[0:1, 0:TCH], psB.ap()[0:1, 0:TCH], 1.0 / D))
            breg("B")
            V(lambda h: h.tensor_mul(srep.ap()[0:1, 0:TCH], murep.ap()[0:1, 0:TCH], murep.ap()[0:1, 0:TCH]))
            V(lambda h: h.tensor_sub(ivrep.ap()[0:1, 0:TCH], ivrep.ap()[0:1, 0:TCH], srep.ap()[0:1, 0:TCH]))
            cs = S(lambda h: h.activation(ivrep.ap()[0:1, 0:TCH], ivrep.ap()[0:1, 0:TCH], AF.Sqrt,
                                          bias=smc("eps")[0:1, :]), deps=[("vector", p.now("vector"))])
            cv = V(lambda h: h.reciprocal(ivrep.ap()[0:1, 0:TCH], ivrep.ap()[0:1, 0:TCH]), deps=[("scalar", cs)])
            cb1 = G(lambda h: h.partition_broadcast(murep.ap()[:, 0:TCH], murep.ap()[0:1, 0:TCH], P),
                    deps=[("vector", cv)])
            cb2 = G(lambda h: h.partition_broadcast(ivrep.ap()[:, 0:TCH], ivrep.ap()[0:1, 0:TCH], P),
                    deps=[("vector", cv)])
            for kt in range(KT):
                V(lambda h, kt=kt: h.tensor_sub(pf(TFX, TCH), pb(H2 + kt * TCH, TCH), murep.ap()[:, 0:TCH]),
                  deps=[("gpsimd", cb2)])
                V(lambda h: h.tensor_mul(pf(TFX, TCH), pf(TFX, TCH), ivrep.ap()[:, 0:TCH]))
                V(lambda h, kt=kt: h.tensor_scalar(
                    pb(XLN + kt * TCH, TCH), pf(TFX, TCH), smc("ln1g", kt), smc("ln1b", kt), ALU.mult, ALU.add))
            cab0 = S(lambda h: h.activation(pf(ABSA, TCH), pb(XLN, TCH), AF.Abs),
                     deps=[("vector", p.now("vector"))])
            cacc = V(lambda h: h.tensor_copy(pf(ABSC, TCH), pf(ABSA, TCH)), deps=[("scalar", cab0)])
            for kt in range(1, KT):
                cab = S(lambda h, kt=kt: h.activation(pf(ABSA, TCH), pb(XLN + kt * TCH, TCH), AF.Abs),
                        deps=[("vector", cacc)])
                cacc = V(lambda h: h.tensor_max(pf(ABSC, TCH), pf(ABSC, TCH), pf(ABSA, TCH)),
                         deps=[("scalar", cab)])
            c_absdone = p.now("scalar")
            cgm = G(lambda h: h.partition_all_reduce(pf(ABSA, TCH), pf(ABSC, TCH), P, RED.max),
                    deps=[("vector", cacc)])
            V(lambda h: h.tensor_scalar_max(murep.ap()[0:1, 0:TCH], pf(ABSA, TCH)[0:1, :], 1e-5),
              deps=[("gpsimd", cgm)])
            V(lambda h: h.reciprocal(srep.ap()[0:1, 0:TCH], murep.ap()[0:1, 0:TCH]))
            cs1 = V(lambda h: h.tensor_scalar_mul(srep.ap()[0:1, 0:TCH], srep.ap()[0:1, 0:TCH], 127.0))
            V(lambda h: h.tensor_scalar_mul(murep.ap()[0:1, 0:TCH], murep.ap()[0:1, 0:TCH], 1.0 / 127.0))
            cr1 = V(lambda h: h.tensor_single_scalar(murep.ap()[0:1, 0:TCH], murep.ap()[0:1, 0:TCH],
                                                     svec.ap()[0:1, 2:3], ALU.mult))
            cbs = G(lambda h: h.partition_broadcast(srep.ap()[:, 0:TCH], srep.ap()[0:1, 0:TCH], P),
                    deps=[("vector", cs1)])
            _PH[(tch, "quant")] = p.idx
            cq1 = V(lambda h: h._custom_dve(
                ops["ZZACTQ"], out=pb(QI1, KT * TCH).rearrange("p (k c) -> p k c", k=KT),
                in0=pb(XLN, KT * TCH).rearrange("p (k c) -> p k c", k=KT),
                in1=srep.ap()[:, 0:TCH].rearrange("p (o n) -> p o n", o=1).broadcast_to([P, KT, TCH]),
                s0=MAGIC, s1=-128.0), deps=[("gpsimd", cbs)])
            for tt2 in range(TCH // P):
                strip_transpose(murep.ap()[0:1, tt2 * P:(tt2 + 1) * P], svec.ap()[:, 56 + tt2:57 + tt2],
                                deps=[("vector", cr1)])
            c_r1T = p.now("vector")
            _PH[(tch, "ffn1")] = p.idx
            gacc = svec.ap()[:, 8:24]
            cwds = {}
            def emit_wdma(r):
                b = WS1 if r % 2 == 0 else WS1B
                q, qs = ("sync", "sync") if r % 2 == 0 else ("scalar", "scdma")
                deps = [(qs, p.now(qs))]
                if b in wrd:
                    deps.append(("tensor", wrd[b]))
                elif tch == 0:
                    deps.append(("gpsimd", c_reps))
                if b == WS1B:
                    deps += [("vector", cq1), ("scalar", c_absdone), ("gpsimd", cgm)]
                if r < 8:
                    cwds[r] = (qs, dma(q, pb(b, 4096).rearrange("p (k c) -> p k c", k=KT),
                                       sc_w1.rearrange("(k p) n -> p k n", p=P)[:, :, r * 512:(r + 1) * 512],
                                       deps=deps))
                else:
                    cwds[r] = (qs, dma(q, pb(b, 4096).rearrange("p (k c) -> p k c", k=32),
                                       sc_w2.rearrange("(k p) n -> p k n", p=P)[:, :, (r - 8) * P:(r - 7) * P],
                                       deps=deps))
            emit_wdma(0)
            emit_wdma(1)
            for m in range(KT):
                wb = WS1 if m % 2 == 0 else WS1B
                for tt2 in range(TCH // P):
                    ps = psA if tt2 == 0 else psB
                    bnk = "A" if tt2 == 0 else "B"
                    for kt in range(KT):
                        TT(lambda h, tt2=tt2, kt=kt, ps=ps, wb=wb: h.matmul(
                            ps.ap()[:], pb(QI1 + kt * TCH + tt2 * P, P), pb(wb + kt * 512, 512),
                            start=(kt == 0), stop=(kt == KT - 1), skip_group_check=True),
                           deps=((bdep(bnk) + [("vector", cq1), cwds[m]]) if kt == 0 else []))
                    cmm = p.now("tensor")
                    cev = V(lambda h, tt2=tt2, ps=ps: h.tensor_single_scalar(
                        pf(TFX, 512), ps.ap()[:], svec.ap()[:, 56 + tt2:57 + tt2], ALU.mult),
                      deps=[("tensor", cmm), ("vector", c_r1T), ("scalar", p.now("scalar"))], sw=False)
                    breg(bnk)
                    V(lambda h, m=m: h.tensor_add(pf(TFX, 512), pf(TFX, 512), pb(RB1 + m * 512, 512)),
                      sw=cev)
                    S(lambda h, m=m, tt2=tt2: h.activation(pb(H4A + tt2 * 4096 + m * 512, 512), pf(TFX, 512),
                                                           AF.Gelu, accum_out=gacc[:, tt2 * 8 + m:tt2 * 8 + m + 1]),
                      deps=[("vector", p.now("vector"))])
                wrd[wb] = p.now("tensor")
                emit_wdma(m + 2)
            _PH[(tch, "ln2")] = p.idx
            c_gelu = p.now("scalar")
            for tt2 in range(TCH // P):
                cs2 = S(lambda h, tt2=tt2: h.activation(pb(SQS, 4096), pb(H4A + tt2 * 4096, 4096), AF.Square,
                                                        accum_out=svec.ap()[:, 40:41]),
                        deps=[("scalar", c_gelu), ("vector", p.now("vector"))])
                cvv = V(lambda h, tt2=tt2: h.reduce_sum(svec.ap()[:, 42:43], gacc[:, tt2 * 8:tt2 * 8 + 8], axis=AXX),
                        deps=[("scalar", c_gelu)])
                V(lambda h: h.tensor_scalar_mul(svec.ap()[:, 44:45], svec.ap()[:, 42:43], 1.0 / 4096),
                  deps=[("vector", cvv)])
                V(lambda h: h.tensor_scalar_mul(svec.ap()[:, 45:46], svec.ap()[:, 40:41], 1.0 / 4096),
                  deps=[("scalar", cs2)])
                V(lambda h: h.tensor_mul(svec.ap()[:, 46:47], svec.ap()[:, 44:45], svec.ap()[:, 44:45]))
                V(lambda h: h.tensor_sub(svec.ap()[:, 45:46], svec.ap()[:, 45:46], svec.ap()[:, 46:47]))
                cs3 = S(lambda h: h.activation(svec.ap()[:, 45:46], svec.ap()[:, 45:46], AF.Sqrt, bias=smc("eps")),
                        deps=[("vector", p.now("vector"))])
                V(lambda h: h.reciprocal(svec.ap()[:, 45:46], svec.ap()[:, 45:46]), deps=[("scalar", cs3)])
                V(lambda h: h.tensor_mul(svec.ap()[:, 46:47], svec.ap()[:, 44:45], svec.ap()[:, 45:46]))
                cng = V(lambda h: h.tensor_scalar_mul(svec.ap()[:, 46:47], svec.ap()[:, 46:47], -1.0))
                V(lambda h, tt2=tt2: h.tensor_scalar(pb(SQS, 4096), pb(H4A + tt2 * 4096, 4096),
                                                     svec.ap()[:, 45:46], svec.ap()[:, 46:47], ALU.mult, ALU.add),
                  deps=[("vector", cng)])
                V(lambda h: h.tensor_mul(pb(SQS, 4096), pb(SQS, 4096), pb(RG2, 4096)))
                czf = V(lambda h: h.tensor_add(pb(SQS, 4096), pb(SQS, 4096), pb(RB2, 4096)))
                V(lambda h: h.reduce_max(svec.ap()[:, 44:45], pb(SQS, 4096), axis=AXX, apply_absolute_value=True),
                  deps=[("vector", czf)])
                V(lambda h: h.tensor_scalar_max(svec.ap()[:, 44:45], svec.ap()[:, 44:45], 1e-5))
                V(lambda h: h.reciprocal(svec.ap()[:, 45:46], svec.ap()[:, 44:45]))
                cs1q = V(lambda h: h.tensor_scalar_mul(svec.ap()[:, 45:46], svec.ap()[:, 45:46], 127.0))
                V(lambda h: h.tensor_scalar_mul(svec.ap()[:, 44:45], svec.ap()[:, 44:45], 1.0 / 127.0))
                cdq = V(lambda h: h.tensor_single_scalar(svec.ap()[:, 44:45], svec.ap()[:, 44:45],
                                                         svec.ap()[:, 3:4], ALU.mult))
                cq2 = V(lambda h: h._custom_dve(ops["ZZACTQ1"], out=pb(Q2A, 4096), in0=pb(SQS, 4096),
                                                s0=svec.ap()[:, 45:46], s1=MAGIC, imm2=127.0), deps=[("vector", cs1q)])
                cq2d = V(lambda h: h.tensor_single_scalar(pb(Q2A, 4096), pb(Q2A, 4096),
                                                          svec.ap()[:, 44:45], ALU.mult), deps=[("vector", cdq)])
                for kt in range(32):
                    transpose2(pb(Q2A + kt * P, P), P, P, pb(Q2F + kt * TCH + tt2 * P, P),
                               deps=[("vector", cq2d)])
            _PH[(tch, "ffn2")] = p.idx
            c_q2f = p.now("vector")
            for m in range(KT):
                r = m + 8
                wb = WS1 if r % 2 == 0 else WS1B
                for kt in range(32):
                    TT(lambda h, m=m, kt=kt, wb=wb: h.matmul(
                        psC.ap()[:, 0:TCH], pb(wb + kt * P, P), pb(Q2F + kt * TCH, TCH),
                        start=(kt == 0), stop=(kt == 31), skip_group_check=True),
                       deps=((bdep("C") + [("vector", c_q2f), cwds[r]]) if kt == 0 else []))
                cmm = p.now("tensor")
                cev = V(lambda h, m=m: h.tensor_scalar(pf(TFX, TCH), psC.ap()[:, 0:TCH], smc("b2", m),
                                                 modP.ap()[:, 5 * 8 + m:5 * 8 + m + 1], ALU.add, ALU.mult),
                  deps=[("tensor", cmm)], sw=False)
                breg("C")
                V(lambda h, m=m, cc=cc: h.tensor_add(xslice(m, cc, TCH), xslice(m, cc, TCH), pf(TFX, TCH)),
                  sw=cev)
                wrd[wb] = p.now("tensor")
                if r + 2 < 16:
                    emit_wdma(r + 2)

        _PH[3] = p.idx
        if _STOP[0] <= 3:
            return finish_now()
        tap("xl0")
        if _TRUNC[0]:
            c_fin0 = p.now("vector")
            for kt in range(KT):
                dma("sync", outT.rearrange("(k p) n -> p k n", p=P)[:, kt:kt + 1, :],
                    xr[:, kt * T + OWN0:kt * T + OWN1].rearrange("p (o n) -> p o n", o=1),
                    deps=[("vector", c_fin0)])
            p.emit("gpsimd", lambda h: h.engine_nop(), deps=[("sync", p.now("sync"))], inc=0)
            p.build(block)
            import concourse.mybir as _mb
            _mb.codegen_inst_isa_subclasses(nc)
            return nc

        c_mod1 = do_mod(wmod1, "c1T", 1)
        KB, VT = 0, 20480
        H3C, VS, VSF = 41440, 45536, 46560
        SQX3 = 48640
        cmz = V(lambda h: h.memset(pb(VT, 20800), 0.0),
                deps=[("tensor", p.now("tensor")), ("sync", p.now("sync"))])
        c_vtz = V(lambda h: h.memset(pb(VT, 20800).rearrange("p (b o) -> p b o", o=65)[:, :, 64:65], 1.0))
        for j in range(NCH):
            cc = j * 512
            chv = ln_chunk(cc, 512, H3C, 48 + 1 * 8, 48 + 0 * 8, SQX3, extra=[("vector", c_mod1)])
            for tile in range(16):
                buf = wba if tile % 2 == 0 else wbb
                q, qs = ("gpsimd", "gpdma") if tile % 2 == 0 else ("sync", "sync")
                cwd = (qs, dma(q, buf.ap()[:].rearrange("p (k c) -> p k c", k=KT),
                          sc_qkv.rearrange("(k p) n -> p k n", p=P)[:, :, 1024 + tile * P:1024 + (tile + 1) * P],
                          deps=[("tensor", p.now("tensor")) if tile >= 2 else ("vector", chv)] +
                               ([("sync", c_castdone)] if tile < 2 else [])))
                ps = psA if tile % 2 == 0 else psB
                bnk = "A" if tile % 2 == 0 else "B"
                for kt in range(KT):
                    TT(lambda h, kt=kt, buf=buf, ps=ps: h.matmul(
                        ps.ap()[:], buf.ap()[:, kt * P:(kt + 1) * P], pb(H3C + kt * 512, 512),
                        start=(kt == 0), stop=(kt == KT - 1), skip_group_check=True),
                       deps=((bdep(bnk) + [("vector", chv), cwd]) if kt == 0 else []))
                cmm = p.now("tensor")
                if tile < 8:
                    V(lambda h, tile=tile, cc=cc, ps=ps: h.scalar_tensor_tensor(
                        pb(KB + tile * T + cc, 512), ps.ap()[:], smc("b1qkv", 8 + tile),
                        kvm.ap()[:, cc:cc + 512], ALU.add, ALU.mult), deps=[("tensor", cmm)], sw=False)
                    breg(bnk)
                else:
                    for s2 in range(4):
                        cve = V(lambda h, tile=tile, cc=cc, ps=ps, s2=s2: h.scalar_tensor_tensor(
                            pb(VS, P), ps.ap()[:, s2 * P:(s2 + 1) * P], smc("b1qkv", 8 + tile),
                            kvm.ap()[:, cc + s2 * P:cc + (s2 + 1) * P], ALU.add, ALU.mult),
                            deps=[("tensor", p.now("tensor"))], sw=False)
                        breg(bnk)
                        for e2 in range(2):
                            B = j * 4 + s2
                            hh = (tile - 8) * 2 + e2
                            transpose2(pb(VS, P)[e2 * 64:e2 * 64 + 64, :], 64, P,
                                       pb(VT + (B * 16 + hh) * 65, 64), deps=[("vector", max(cve, c_vtz))])
        if _STOP[0] == 35:
            return finish_now()
        c_kvdone = p.now("vector")
        atrwar = {0: 0, 1: 0}
        H3B, QCB, ATR, OCB, SQW = 41440, 45536, 46048, 47584, 45536
        for qc in range(4):
            cc0 = OWN0 + qc * 512
            chv = ln_chunk(cc0, 512, H3B, 48 + 1 * 8, 48 + 0 * 8, SQW, extra=[])
            for hp in range(KT):
                buf = wba if hp % 2 == 0 else wbb
                q, qs = ("gpsimd", "gpdma") if hp % 2 == 0 else ("sync", "sync")
                cwd = (qs, dma(q, buf.ap()[:].rearrange("p (k c) -> p k c", k=KT),
                          sc_qkv.rearrange("(k p) n -> p k n", p=P)[:, :, hp * P:(hp + 1) * P],
                          deps=[("tensor", p.now("tensor")) if hp >= 2 else ("vector", chv)]))
                for kt in range(KT):
                    TT(lambda h, kt=kt, buf=buf: h.matmul(
                        psC.ap()[:], buf.ap()[:, kt * P:(kt + 1) * P], pb(H3B + kt * 512, 512),
                        start=(kt == 0), stop=(kt == KT - 1), skip_group_check=True),
                       deps=((bdep("C") + [("vector", chv), cwd]) if kt == 0 else []))
                cq = V(lambda h, hp=hp: h.tensor_scalar_add(pb(QCB, 512), psC.ap()[:], smc("b1qkv", hp)),
                       deps=[("tensor", p.now("tensor"))], sw=False)
                breg("C")
                def emit_pv(st):
                    slot, hh, Q, e2, Qt, atr, cex, par = st
                    for t5 in range(5):
                        TT(lambda h, hh=hh, t5=t5, slot=slot, Q=Q, atr=atr: h.matmul(
                            psO.ap()[:, slot * P:slot * P + P],
                            pb(VT + ((Q // P + t5) * 16 + hh) * 65, 65),
                            pb(atr + t5 * P, P), start=(t5 == 0), stop=(t5 == 4), skip_group_check=True),
                           deps=((bdep("O") + [("scalar", cex), ("vector", c_kvdone)]) if t5 == 0 else []))
                    cpv = p.now("tensor")
                    atrwar[par] = cpv
                    V(lambda h, slot=slot, e2=e2, Qt=Qt: h.tensor_copy(
                        (murep if e2 == 0 else ivrep).ap()[0:1, Qt * P:(Qt + 1) * P],
                        psO.ap()[64:65, slot * P:slot * P + P]),
                      deps=[("tensor", cpv), ("gpsimd", p.now("gpsimd"))], sw=False)
                    breg("O")
                pend = None
                for e2 in range(2):
                    hh = hp * 2 + e2
                    for Qt in range(4):
                        Q = qc * 512 + Qt * P
                        slot = e2 * 4 + Qt
                        par = slot % 2
                        if par == 0:
                            sdst = [psSC.ap()[:, t5 * P:(t5 + 1) * P] for t5 in range(5)]
                            sbanks = ("SC0", "SC1")
                            atr = ATR
                        else:
                            sdst = [psC.ap()[:, t5 * P:(t5 + 1) * P] for t5 in range(4)] + [psD.ap()[:, 0:P]]
                            sbanks = ("C", "D")
                            atr = ATR + 640
                        for t5 in range(5):
                            TT(lambda h, hh=hh, t5=t5, e2=e2, Q=Q, Qt=Qt, sdst=sdst: h.matmul(
                                sdst[t5],
                                pool.ap()[:, KB + (hh // 2) * T + Q + t5 * P: KB + (hh // 2) * T + Q + (t5 + 1) * P][e2 * 64:e2 * 64 + 64, :],
                                pb(QCB + Qt * P, P)[e2 * 64:e2 * 64 + 64, :],
                                start=True, stop=True, skip_group_check=True),
                               deps=((bdep(*sbanks) + [("vector", cq)]) if t5 == 0 else []))
                        csc = p.now("tensor")
                        if pend is not None:
                            emit_pv(pend)
                        V(lambda h, sdst=sdst: h.tensor_add(sdst[0], sdst[0], cmk.ap()[:, 0:P]),
                          deps=[("tensor", csc)], sw=False)
                        breg(sbanks[0])
                        cm4 = V(lambda h, sdst=sdst: h.tensor_add(sdst[4], sdst[4], cmk.ap()[:, P:2 * P]),
                                sw=False)
                        breg(sbanks[1])
                        if par == 0:
                            cex = S(lambda h, atr=atr: h.activation(pb(atr, 640), psSC.ap()[:, 0:640], AF.Exp,
                                                                    bias=smc("negc"), scale=0.125),
                                    deps=[("vector", cm4), ("tensor", atrwar[0])])
                        else:
                            S(lambda h, atr=atr: h.activation(pb(atr, 512), psC.ap()[:, 0:512], AF.Exp,
                                                              bias=smc("negc"), scale=0.125),
                              deps=[("vector", cm4), ("tensor", atrwar[1])])
                            cex = S(lambda h, atr=atr: h.activation(pb(atr + 512, P), psD.ap()[:, 0:P], AF.Exp,
                                                                    bias=smc("negc"), scale=0.125))
                        breg(sbanks[0], "scalar", cex)
                        breg(sbanks[1], "scalar", cex)
                        pend = (slot, hh, Q, e2, Qt, atr, cex, par)
                emit_pv(pend)
                V(lambda h: h.reciprocal(murep.ap()[0:1, :], murep.ap()[0:1, :]))
                crc = V(lambda h: h.reciprocal(ivrep.ap()[0:1, :], ivrep.ap()[0:1, :]))
                cb0 = G(lambda h: h.partition_broadcast(murep.ap()[0:64, :], murep.ap()[0:1, :], 64),
                        deps=[("vector", crc)])
                cb1 = G(lambda h: h.partition_broadcast(ivrep.ap()[0:64, :], ivrep.ap()[0:1, :], 64),
                        deps=[("vector", crc)])
                for e2 in range(2):
                    rep = murep if e2 == 0 else ivrep
                    for Qt in range(4):
                        slot = e2 * 4 + Qt
                        V(lambda h, e2=e2, Qt=Qt, slot=slot, rep=rep, hp=hp: h.tensor_tensor(
                            pool.ap()[:, OCB + hp * 512 + Qt * P: OCB + hp * 512 + (Qt + 1) * P][e2 * 64:e2 * 64 + 64, :],
                            psO.ap()[0:64, slot * P:slot * P + P], rep.ap()[0:64, Qt * P:(Qt + 1) * P], ALU.mult),
                          deps=[("gpsimd", cb1)], sw=False)
                        breg("O")
            c_oc = p.now("vector")
            for m in range(KT):
                buf = wba if m % 2 == 0 else wbb
                q, qs = ("gpsimd", "gpdma") if m % 2 == 0 else ("sync", "sync")
                cwd = (qs, dma(q, buf.ap()[:].rearrange("p (k c) -> p k c", k=KT),
                          sc_o.rearrange("(k p) n -> p k n", p=P)[:, :, m * P:(m + 1) * P],
                          deps=[("tensor", p.now("tensor")) if m >= 2 else ("vector", c_oc)]))
                for kt in range(KT):
                    TT(lambda h, m=m, kt=kt, buf=buf: h.matmul(
                        psC.ap()[:], buf.ap()[:, kt * P:(kt + 1) * P],
                        pool.ap()[:, OCB + kt * 512:OCB + (kt + 1) * 512],
                        start=(kt == 0), stop=(kt == KT - 1), skip_group_check=True),
                       deps=((bdep("C") + [("vector", c_oc), cwd]) if kt == 0 else []))
                cmm = p.now("tensor")
                cev = V(lambda h, m=m: h.tensor_scalar(pf(SQW, 512), psC.ap()[:], smc("b1o", m),
                                                 modP.ap()[:, 48 + 2 * 8 + m:48 + 2 * 8 + m + 1], ALU.add, ALU.mult),
                  deps=[("tensor", cmm)], sw=False)
                breg("C")
                V(lambda h, m=m, cc0=cc0: h.tensor_add(xslice(m, cc0, 512), xslice(m, cc0, 512), pf(SQW, 512)),
                  sw=cev)

        _PH[4] = p.idx
        if _STOP[0] <= 4:
            return finish_now()
        tap("xattn1")

        X4, XQ, XTS = 0, 4096, 8192
        H5, F1B, UB, AB5 = 12288, 23552, 26368, 29184
        HQF, SFP, TF4, WG_ = 32000, 43264, 8192, 44320
        WU_ = 48416
        wrd1 = {}
        for qc in range(4):
            cc0 = OWN0 + qc * 512
            chv = ln_chunk(cc0, 512, X4, 48 + 4 * 8, 48 + 3 * 8, TF4,
                           extra=[("tensor", p.now("tensor"))])
            c_xq = 0
            for ttile in range(4):
                cin = p.now("vector")
                for kt in range(KT):
                    transpose2(pb(X4 + kt * 512 + ttile * P, P), P, P, pb(XTS + kt * P, P),
                               deps=[("vector", chv)])
                ctr = p.now("vector")
                cdq = fp4_qdq(XTS, 1024, 64, F1B, UB, AB5, XTS + 1024, SFP, deps=[("vector", ctr), ("tensor", p.now("tensor"))])
                for kt in range(KT):
                    c_xq = transpose2(pb(XTS + 1024 + kt * P, P), P, P, pb(XQ + kt * 512 + ttile * P, P),
                                      deps=[("vector", cdq)])
            c_xqT = p.now("tensor")
            cgs = {}
            def emit_gdma(r):
                if r >= 6:
                    return
                ow = 512 if r < 5 else 256
                g, u = (WG_, WU_) if r % 2 == 0 else (X4, HQF)
                dg = [("sync", p.now("sync")), ("tensor", wrd1.get(g, p.now("tensor")))]
                if g == X4:
                    dg.append(("tensor", c_xqT))
                cga = dma("sync", pb(g, KT * ow).rearrange("p (k c) -> p k c", k=KT),
                          sc_wg.rearrange("(k p) n -> p k n", p=P)[:, :, r * 512:r * 512 + ow], deps=dg)
                cgb = dma("scalar", pb(u, KT * ow).rearrange("p (k c) -> p k c", k=KT),
                          sc_wu.rearrange("(k p) n -> p k n", p=P)[:, :, r * 512:r * 512 + ow],
                          deps=[("tensor", wrd1.get(u, p.now("tensor"))), ("scdma", p.now("scdma"))])
                cgs[r] = (cga, cgb)
            emit_gdma(0)
            emit_gdma(1)
            for och in range(6):
                ow = 512 if och < 5 else 256
                c0 = och * 512
                g, u = (WG_, WU_) if och % 2 == 0 else (X4, HQF)
                cg1, cg2 = cgs[och]
                for ttile in range(4):
                    for kt in range(KT):
                        TT(lambda h, kt=kt, ttile=ttile, ow=ow, g=g: h.matmul(
                            psA.ap()[:, 0:ow], pb(XQ + kt * 512 + ttile * P, P), pb(g + kt * ow, ow),
                            start=(kt == 0), stop=(kt == KT - 1), skip_group_check=True),
                           deps=((bdep("A") + [("vector", c_xq), ("sync", cg1)]) if kt == 0 else []))
                    for kt in range(KT):
                        TT(lambda h, kt=kt, ttile=ttile, ow=ow, u=u: h.matmul(
                            psB.ap()[:, 0:ow], pb(XQ + kt * 512 + ttile * P, P), pb(u + kt * ow, ow),
                            start=(kt == 0), stop=(kt == KT - 1), skip_group_check=True),
                           deps=((bdep("B") + [("scdma", cg2)]) if kt == 0 else []))
                    cmm = p.now("tensor")
                    csl = S(lambda h, ow=ow: h.activation(pb(XTS, ow), psA.ap()[:, 0:ow], AF.Silu),
                            deps=[("tensor", cmm), ("vector", p.now("vector"))])
                    breg("A", "scalar", csl)
                    V(lambda h, ttile=ttile, c0=c0, ow=ow: h.tensor_tensor(
                        pb(H5 + ttile * 2816 + c0, ow), pb(XTS, ow), psB.ap()[:, 0:ow], ALU.mult),
                      deps=[("scalar", csl)], sw=False)
                    breg("B")
                wrd1[g] = wrd1[u] = p.now("tensor")
                emit_gdma(och + 2)
            c_h5 = p.now("vector")
            cwds2 = {}
            def emit_ddma(r):
                if r >= 8:
                    return
                b = WG_ if r % 2 == 0 else X4
                q, qs = ("sync", "sync") if r % 2 == 0 else ("scalar", "scdma")
                cwds2[r] = (qs, dma(q, pb(b, 22 * P).rearrange("p (k c) -> p k c", k=22),
                                    sc_wd.rearrange("(k p) n -> p k n", p=P)[:, :, r * P:(r + 1) * P],
                                    deps=[(qs, p.now(qs)), ("tensor", wrd1.get(b, p.now("tensor")))]))
            emit_ddma(0)
            emit_ddma(1)
            for ttile in range(4):
                cdq = fp4_qdq(H5 + ttile * 2816, 2816, 176, F1B, UB, AB5, F1B, SFP, deps=[("vector", c_h5), ("tensor", p.now("tensor"))])
                for kt in range(22):
                    transpose2(pb(F1B + kt * P, P), P, P, pb(HQF + kt * 512 + ttile * P, P),
                               deps=[("vector", cdq)])
            c_hq = p.now("vector")
            for m in range(KT):
                b = WG_ if m % 2 == 0 else X4
                for kt in range(22):
                    TT(lambda h, m=m, kt=kt, b=b: h.matmul(
                        psC.ap()[:], pb(b + kt * P, P), pb(HQF + kt * 512, 512),
                        start=(kt == 0), stop=(kt == 21), skip_group_check=True),
                       deps=((bdep("C") + [("vector", c_hq), cwds2[m]]) if kt == 0 else []))
                cmm = p.now("tensor")
                V(lambda h, m=m, cc0=cc0: h.scalar_tensor_tensor(
                    xslice(m, cc0, 512), psC.ap()[:], modP.ap()[:, 48 + 5 * 8 + m:48 + 5 * 8 + m + 1],
                    xslice(m, cc0, 512), ALU.mult, ALU.add),
                  deps=[("tensor", cmm)])
                breg("C")
                wrd1[b] = p.now("tensor")
                emit_ddma(m + 2)

        _PH[9] = p.idx
        c_fin = p.now("vector")
        for kt in range(KT):
            dma("sync", outT.rearrange("(k p) n -> p k n", p=P)[:, kt:kt + 1, :],
                xr[:, kt * T + OWN0:kt * T + OWN1].rearrange("p (o n) -> p o n", o=1),
                deps=[("vector", c_fin)])
        c_out = p.now("sync")
        p.emit("gpsimd", lambda h: h.engine_nop(), deps=[("sync", c_out)], inc=0)

        p.build(block)
    mybir.codegen_inst_isa_subclasses(nc)
    return nc


_NC_CACHE = {}


def _host_pack(inputs):
    f32 = np.float32
    ins = {k: np.asarray(v) for k, v in inputs.items()}
    x = ins["L0_attn_query"]

    def t(a):
        return np.ascontiguousarray(a.T.astype(f32))

    Wqkv0 = ins["L0_attn_W_qkv"]
    shared = {
        "w0qvT": np.ascontiguousarray(np.concatenate([Wqkv0[0:1024], Wqkv0[2048:3072]], 0).T.astype(f32)),
        "w0oT": t(ins["L0_attn_W_out"]),
        "w1T": t(ins["L0_ffn_w1"]),
        "w2T": t(ins["L0_ffn_w2"]),
        "w1qkvT": t(ins["L1_attn_Wqkv"]),
        "w1oT": t(ins["L1_attn_Wout"]),
        "wgO": np.ascontiguousarray(ins["L1_ffn_w_gate"].astype(f32)),
        "wuO": np.ascontiguousarray(ins["L1_ffn_w_up"].astype(f32)),
        "wdO": np.ascontiguousarray(ins["L1_ffn_w_down"].astype(f32)),
        "wmod0": np.ascontiguousarray(ins["L0_wmod"].astype(f32)),
        "wmod1": np.ascontiguousarray(ins["L1_wmod"].astype(f32)),
        "identb": np.eye(P).astype(ml_dtypes.bfloat16),
        "strps": np.concatenate([ins["L0_ffn_b1"], ins["L0_ffn_ln2_g"], ins["L0_ffn_ln2_b"]]).astype(f32)[None, :],
    }
    cm = np.zeros((P, 256), f32)
    ii = np.arange(P)[:, None]
    rr = np.arange(P)[None, :]
    cm[:, 0:P] = np.where(ii >= rr, 0.0, -1e9)
    cm[:, P:2 * P] = np.where(ii < rr, 0.0, -1e9)
    shared["cmask"] = cm

    def colP(v):
        return np.ascontiguousarray(np.asarray(v, f32).reshape(-1, P).T)

    SM = {}
    _c = [0]
    for nm, n in [("c0T", 8), ("c1T", 8), ("b0qv", 16), ("b0o", 8), ("wqP", 24), ("wvP", 24),
                  ("ln1g", 8), ("ln1b", 8), ("b2", 8), ("b1qkv", 24), ("b1o", 8),
                  ("negc", 1), ("eps", 1)]:
        SM[nm] = _c[0]; _c[0] += n

    cores = []
    for c in range(8):
        b, half = c // 2, c % 2
        xTv = np.zeros((D, T), f32)
        kv = np.zeros((P, T), ml_dtypes.bfloat16)
        if half == 0:
            xTv[:, 256:T] = x[b, 0:2304].T
            kv[:, 256:T] = 1.0
        else:
            xTv[:, 0:2304] = x[b, 1792:4096].T
            kv[:, 0:2304] = 1.0
        smal = np.zeros((P, 160), f32)
        smal[:, SM["c0T"]:SM["c0T"] + 8] = colP(ins["L0_c"][b])
        smal[:, SM["c1T"]:SM["c1T"] + 8] = colP(ins["L1_c"][b])
        bqv = np.concatenate([ins["L0_attn_b_qkv"][0:1024], ins["L0_attn_b_qkv"][2048:3072]])
        smal[:, SM["b0qv"]:SM["b0qv"] + 16] = colP(bqv)
        smal[:, SM["b0o"]:SM["b0o"] + 8] = colP(ins["L0_attn_b_out"])
        for j in range(3):
            smal[:, SM["wqP"] + 8 * j:SM["wqP"] + 8 * (j + 1)] = colP(ins["L0_attn_weight_q"][:, j, :].reshape(-1))
            smal[:, SM["wvP"] + 8 * j:SM["wvP"] + 8 * (j + 1)] = colP(ins["L0_attn_weight_v"][:, j, :].reshape(-1))
        smal[:, SM["ln1g"]:SM["ln1g"] + 8] = colP(ins["L0_ffn_ln1_g"])
        smal[:, SM["ln1b"]:SM["ln1b"] + 8] = colP(ins["L0_ffn_ln1_b"])
        smal[:, SM["b2"]:SM["b2"] + 8] = colP(ins["L0_ffn_b2"])
        smal[:, SM["b1qkv"]:SM["b1qkv"] + 24] = colP(ins["L1_attn_bqkv"])
        smal[:, SM["b1o"]:SM["b1o"] + 8] = colP(ins["L1_attn_bout"])
        smal[:, SM["negc"]] = -CSHIFT
        smal[:, SM["eps"]] = LNEPS
        d = {"xT": xTv, "kvmask": kv, "smal": smal}
        d.update(shared)
        cores.append(d)
    return cores


def kernel(**inputs):
    taps = tuple(_TAPS)
    if taps not in _NC_CACHE:
        _NC_CACHE[taps] = build_nc(taps)
    nc = _NC_CACHE[taps]
    in_maps = _host_pack(inputs)
    res = run_bass_kernel_spmd(nc, in_maps, core_ids=list(range(8)))
    out = np.zeros((4, 4096, D), np.float32)
    for c in range(8):
        b, half = c // 2, c % 2
        out[b, half * 2048:(half + 1) * 2048, :] = np.asarray(res.results[c]["outT"]).T
    if taps:
        kernel._tapres = [{t: np.asarray(res.results[c]["tap_" + t]) for t in taps} for c in range(8)]
    return out


# version 06 (final)
# speedup vs baseline: 1.1596x; 1.1596x over previous
import numpy as np
import ml_dtypes
import concourse.bass as bass
import concourse.bass_isa as bass_isa
import concourse.mybir as mybir
from concourse.bass_utils import run_bass_kernel_spmd
from concourse.library_config import attn as attn_lib

F32 = mybir.dt.float32
BF16 = mybir.dt.bfloat16
AF = mybir.ActivationFunctionType
ALU = mybir.AluOpType
RED = bass_isa.ReduceOp
AXX = mybir.AxisListType.X

P = 128
T = 2560
OWN0, OWN1 = 256, 2304
NOWN = OWN1 - OWN0
D = 1024
KT = 8
NCH = T // 512
MAGIC = 12582912.0
CSHIFT = 40.0
LNEPS = 1e-5

_TAPS = []
_TRUNC = [False]
_STOP = [99]
_PARANOID = [False]
_PSCOPE = [0, 0]
_PH = {}
_RACEDET = [False]
_PSRANGES = []
_OPSCACHE = {}


def _register_ops():
    if _OPSCACHE:
        return _OPSCACHE
    from concourse.dve_spec import (Spec, Src0, Src1, C0, C1, C2, Zero, One,
                                    maxx, minn, lower, Latch)
    from concourse.dve_op import DveOp
    from concourse import dve_ops

    defs = []
    _r = ((Src0 * Src1) + C0) - C0
    _lo = maxx(_r, C1)
    _hi = Zero - maxx(Zero - _lo, Latch(C1 + One))
    defs.append(("ZZACTQ", Spec(body=_hi),
                 lambda in0, in1, s0, s1, imm2: np.minimum(np.maximum(np.round(in0 * in1), s1), -(s1 + 1.0))))
    _r1 = ((Src0 * C0) + C1) - C1
    _b1 = minn(maxx(_r1, Zero - Latch(C2 + One)), C2)
    defs.append(("ZZACTQ1", Spec(body=_b1),
                 lambda in0, in1, s0, s1, imm2: np.minimum(np.maximum(np.round(in0 * s0), -(imm2 + 1.0)), imm2)))
    _r2 = ((Src0 * C0) + C1) - C1
    defs.append(("ZZWQNT", Spec(body=minn(maxx(_r2, Latch(Zero - One)), One)),
                 lambda in0, in1, s0, s1, imm2: np.minimum(np.maximum(np.round(in0 * s0), -1.0), 1.0)))
    _t2 = Src0 * Src1
    _f = (minn(maxx(_t2, Latch(Zero - C0)), C0) + C1) - C1
    defs.append(("ZZFP4A", Spec(body=_f),
                 lambda in0, in1, s0, s1, imm2: np.round(np.clip(in0 * in1, -s0, s0))))
    _b = (Src0 * Src1) - C0
    _c10 = _b > C0
    defs.append(("ZZFP4B", Spec(body=((_b > Zero) + (_b > C1)) + (_c10 + _c10)),
                 lambda in0, in1, s0, s1, imm2: (
                     ((in0 * in1 - s0) > 0.0).astype(np.float64) + ((in0 * in1 - s0) > s1)
                     + 2.0 * ((in0 * in1 - s0) > s0))))

    n = len(dve_ops.OPS)
    for i, (name, spec, reffn) in enumerate(defs):
        idx = n - len(defs) + i
        shas = {v: lower(spec, ver=v, name=name).sha(v) for v in ("v3", "v4")}
        op = DveOp(name, spec, subdim=False, uops_sha=shas, reference=reffn)
        old = dve_ops.OPS[idx]
        dve_ops.OPS[idx] = op
        dve_ops.CUSTOM_DVE_SPECS.pop(old.name, None)
        dve_ops.CUSTOM_DVE_REFS.pop(old.name, None)
        dve_ops._SUB_OPCODE_FOR_NAME.pop(old.name, None)
        dve_ops.CUSTOM_DVE_SPECS[name] = spec
        dve_ops.CUSTOM_DVE_REFS[name] = reffn
        dve_ops._SUB_OPCODE_FOR_NAME[name] = dve_ops._CUSTOM_DVE_ROW_BASE + idx
        _OPSCACHE[name] = op
    return _OPSCACHE


ENGS = ("sync", "gpsimd", "gpdma", "vector", "scalar", "scdma", "tensor")
QUEUE_OF = {"sync": "sync", "gpsimd": "gpsimd", "gpdma": "gpsimd", "vector": "vector",
            "scalar": "scalar", "scdma": "scalar", "tensor": "tensor"}


class Prog:
    def __init__(self):
        self.q = {en: [] for en in ("sync", "gpsimd", "vector", "scalar", "tensor")}
        self.cnt = {en: 0 for en in ENGS}
        self.hw = {en: {e2: 0 for e2 in ENGS} for en in ("sync", "gpsimd", "vector", "scalar", "tensor")}
        self.sem = {}
        self.idx = 0

    def emit(self, eng, fn, deps=(), inc=1, sw=None):
        self.idx += 1
        if _PARANOID[0] or _PSCOPE[0] <= self.idx < _PSCOPE[1] or \
           any(lo <= self.idx < hi for lo, hi in _PSRANGES):
            deps = list(deps) + [(en, self.cnt[en]) for en in ENGS]
        q = QUEUE_OF[eng]
        real = []
        for (en, c) in deps:
            if QUEUE_OF[en] == q or c <= 0:
                continue
            if c > self.hw[q][en]:
                real.append((en, c))
                self.hw[q][en] = c
        self.q[q].append((tuple(real), fn, inc, eng, sw))
        self.cnt[eng] += inc
        return self.cnt[eng]

    def now(self, eng):
        return self.cnt[eng]

    def build(self, block):
        sem = self.sem
        def mk(qname):
            items = self.q[qname]
            selfser = qname == "vector"
            def body(h):
                prev = 0
                waited = 0
                for deps, fn, inc, semname, sw in items:
                    if selfser and sw is not False:
                        tgt = prev if sw is None else min(sw, prev)
                        if tgt > waited:
                            h.wait_ge(sem[qname], tgt)
                            waited = tgt
                    for (en, c) in deps:
                        h.wait_ge(sem[en], c)
                    r = fn(h)
                    if inc:
                        r.then_inc(sem[semname], inc)
                    if semname == qname:
                        prev += inc
            return body
        for name, fn in (("sync", block.sync), ("gpsimd", block.gpsimd), ("vector", block.vector),
                         ("scalar", block.scalar), ("tensor", block.tensor)):
            if self.q[name]:
                fn(mk(name))


def build_nc(taps=()):
    ops = _register_ops()
    nc = bass.Bass("TRN2", target_bir_lowering=False, detect_race_conditions=_RACEDET[0])

    def din(name, shape, dt=F32):
        return nc.dram_tensor(name, list(shape), dt, kind="ExternalInput").ap()

    xT = din("xT", (D, T))
    kvmask = din("kvmask", (P, T), BF16)
    identb = din("identb", (P, P), BF16)
    cmask = din("cmask", (P, 256))
    smal = din("smal", (P, 160))
    strps = din("strps", (1, 3 * 4096))
    w0qvT = din("w0qvT", (D, 2048))
    w0oT = din("w0oT", (D, D))
    wmod0 = din("wmod0", (D, 6144))
    w1T = din("w1T", (D, 4096))
    w2T = din("w2T", (4096, D))
    w1qkvT = din("w1qkvT", (D, 3072))
    w1oT = din("w1oT", (D, D))
    wgO = din("wgO", (2816, D))
    wuO = din("wuO", (2816, D))
    wdO = din("wdO", (D, 2816))
    wmod1 = din("wmod1", (D, 6144))
    outT = nc.dram_tensor("outT", [D, NOWN], F32, kind="ExternalOutput").ap()
    tapout = {t: nc.dram_tensor("tap_" + t, [D, T], F32, kind="ExternalOutput").ap() for t in taps}

    sc_w1 = nc.dram_tensor("sc_w1", [D, 4096], BF16, kind="Internal").ap()
    sc_w2 = nc.dram_tensor("sc_w2", [4096, D], BF16, kind="Internal").ap()
    sc_wg = nc.dram_tensor("sc_wg", [D, 2816], BF16, kind="Internal").ap()
    sc_wu = nc.dram_tensor("sc_wu", [D, 2816], BF16, kind="Internal").ap()
    sc_wd = nc.dram_tensor("sc_wd", [2816, D], BF16, kind="Internal").ap()
    sc_qkv = nc.dram_tensor("sc_qkv", [D, 3072], BF16, kind="Internal").ap()
    sc_o = nc.dram_tensor("sc_o", [D, D], BF16, kind="Internal").ap()

    SM = {}
    _c = [0]
    for nm, n in [("c0T", 8), ("c1T", 8), ("b0qv", 16), ("b0o", 8), ("wqP", 24), ("wvP", 24),
                  ("ln1g", 8), ("ln1b", 8), ("b2", 8), ("b1qkv", 24), ("b1o", 8),
                  ("negc", 1), ("eps", 1)]:
        SM[nm] = _c[0]; _c[0] += n
    assert _c[0] <= 160

    from contextlib import ExitStack
    with ExitStack() as ctx:
        e = ctx.enter_context
        xres = e(nc.sbuf_tensor("xres", [P, KT * T], F32))
        pool = e(nc.sbuf_tensor("pool", [P, 52600], BF16))
        poolf = pool.ap().bitcast(F32)
        wba = e(nc.sbuf_tensor("wba", [P, 1024], BF16))
        wbb = e(nc.sbuf_tensor("wbb", [P, 1024], BF16))
        sm = e(nc.sbuf_tensor("sm", [P, 160], F32))
        cmk = e(nc.sbuf_tensor("cmk", [P, 256], F32))
        idb = e(nc.sbuf_tensor("idb", [P, P], BF16))
        kvm = e(nc.sbuf_tensor("kvm", [P, T], BF16))
        strip = e(nc.sbuf_tensor("strip", [1, 1024], F32))
        modP = e(nc.sbuf_tensor("modP", [P, 96], F32))
        svec = e(nc.sbuf_tensor("svec", [P, 96], F32))
        onesf = e(nc.sbuf_tensor("onesf", [P, 1], F32))
        onesb = e(nc.sbuf_tensor("onesb", [P, 1], BF16))
        murep = e(nc.sbuf_tensor("murep", [P, 512], F32))
        ivrep = e(nc.sbuf_tensor("ivrep", [P, 512], F32))
        srep = e(nc.sbuf_tensor("srep", [P, 512], F32))
        psA = e(nc.psum_tensor("psA", [P, 512], F32))
        psB = e(nc.psum_tensor("psB", [P, 512], F32))
        psC = e(nc.psum_tensor("psC", [P, 512], F32))
        psSC = e(nc.psum_tensor("psSC", [P, 1024], F32))
        psO = e(nc.psum_tensor("psO", [65, 1024], F32))
        psD = e(nc.psum_tensor("psD", [P, 512], F32))
        psTb = psD.ap().bitcast(BF16)
        psSCb = psSC.ap().bitcast(BF16)
        sems = {en: e(nc.semaphore("s_" + en)) for en in ENGS}
        block = e(nc.Block())

        p = Prog()
        p.sem = sems

        DMASEM = {"gpsimd": "gpdma", "scalar": "scdma"}
        def dma(eng, out, in_, deps=()):
            sname = DMASEM.get(eng, eng)
            return p.emit(sname, lambda h: h.dma_start(out=out, in_=in_), deps, inc=16)
        V = lambda fn, deps=(), sw=None: p.emit("vector", fn, deps, 1, sw)
        S = lambda fn, deps=(): p.emit("scalar", fn, deps, 1)
        G = lambda fn, deps=(): p.emit("gpsimd", fn, deps, 1)
        TT = lambda fn, deps=(): p.emit("tensor", fn, deps, 1)

        def psdep(*extra):
            return [("vector", p.now("vector")), ("scalar", p.now("scalar"))] + list(extra)

        bank_r = {}
        def breg(bank, eng="vector", cnt=None):
            d = bank_r.setdefault(bank, {})
            c = p.now(eng) if cnt is None else cnt
            if c > d.get(eng, 0):
                d[eng] = c
        def bdep(*banks):
            dd = {}
            for b in banks:
                for en, c in bank_r.get(b, {}).items():
                    if c > dd.get(en, 0):
                        dd[en] = c
            return list(dd.items())

        def smc(nm, i0=0, n=1):
            c = SM[nm]
            return sm.ap()[:, c + i0:c + i0 + n]

        def pb(off, n):
            assert off + n <= 52600, (off, n)
            return pool.ap()[:, off:off + n]

        def pf(off, n):
            assert off % 2 == 0 and off + 2 * n <= 52600, (off, n)
            return poolf[:, off // 2:off // 2 + n]

        xr = xres.ap()
        def xslice(kt, c0, n):
            return xr[:, kt * T + c0: kt * T + c0 + n]


        def finish_now():
            c_fin0 = p.now("vector")
            for kt in range(KT):
                dma("sync", outT.rearrange("(k p) n -> p k n", p=P)[:, kt:kt + 1, :],
                    xr[:, kt * T + OWN0:kt * T + OWN1].rearrange("p (o n) -> p o n", o=1),
                    deps=[("vector", c_fin0), ("gpdma", p.now("gpdma")), ("scalar", p.now("scalar")),
                          ("scdma", p.now("scdma")), ("tensor", p.now("tensor")), ("gpsimd", p.now("gpsimd"))])
            p.emit("gpsimd", lambda h: h.engine_nop(), deps=[("sync", p.now("sync"))], inc=0)
            p.build(block)
            import concourse.mybir as _mb
            _mb.codegen_inst_isa_subclasses(nc)
            return nc
        dma("sync", sm.ap()[:], smal)
        dma("sync", cmk.ap()[:], cmask)
        dma("sync", idb.ap()[:], identb)
        dma("sync", kvm.ap()[:], kvmask)
        c_smalls = p.now("sync")
        for k in range(4):
            dma("sync", xr[:, 2 * k * T:(2 * k + 2) * T].rearrange("p (k n) -> p k n", k=2),
                xT.rearrange("(k p) n -> p k n", p=P)[:, 2 * k:2 * k + 2, :])
        c_xload = p.now("sync")
        p.emit("gpsimd", lambda h: h.load_library(attn_lib), deps=(), inc=0)
        c_lib = G(lambda h: h.engine_nop())
        V(lambda h: h.memset(onesf.ap()[:], 1.0))
        c_ones = V(lambda h: h.memset(onesb.ap()[:], 1.0))

        for j, ws in enumerate([2.0 ** -4.5, 2.0 ** -4.5, 2.0 ** -3.0]):
            V(lambda h, j=j: h.tensor_mul(svec.ap()[:, 64 + 8 * j:72 + 8 * j],
                                          smc("wqP", 8 * j, 8), smc("wvP", 8 * j, 8)),
              deps=[("sync", c_smalls)])
            V(lambda h, j=j, ws=ws: h.tensor_scalar_mul(svec.ap()[:, 64 + 8 * j:72 + 8 * j],
                                                        svec.ap()[:, 64 + 8 * j:72 + 8 * j], ws))
        c_wc = p.now("vector")

        _trs = {"i": 0}
        _TRB = ("D", "SC0", "SC1")
        def transpose2(in_ap, np_, nf, dst_ap, deps=()):
            i = _trs["i"] % 3
            _trs["i"] += 1
            sl = [psTb[0:nf, 0:np_], psSCb[0:nf, 0:np_], psSCb[0:nf, 1024:1024 + np_]][i]
            bp = in_ap.base_partition()
            ct = TT(lambda h: h.transpose(sl, in_ap, idb.ap()[bp:bp + np_, bp:bp + np_]),
                    deps=list(deps) + bdep(_TRB[i]))
            cv = V(lambda h: h.tensor_copy(dst_ap, sl), deps=[("tensor", ct)], sw=False)
            breg(_TRB[i], "vector", cv)
            return cv

        def strip_transpose(src_1xN, dst_col, deps=()):
            i = _trs["i"] % 3
            _trs["i"] += 1
            sl = [psD.ap()[:, 0:1], psSC.ap()[:, 0:1], psSC.ap()[:, 512:513]][i]
            ct = TT(lambda h: h.transpose(sl, src_1xN, onesf.ap()[0:1, 0:1]),
                    deps=list(deps) + bdep(_TRB[i]))
            cv = V(lambda h: h.tensor_copy(dst_col, sl), deps=[("tensor", ct)], sw=False)
            breg(_TRB[i], "vector", cv)
            return cv

        def wprep_bit(wsrc, rows, cols, sc_dst, svcol):
            kt = rows // P
            cw = 4096 // kt
            nchunks = cols // cw
            csrc = wsrc.rearrange("(k p) n -> p k n", p=P)
            cdst = sc_dst.rearrange("(k p) n -> p k n", p=P)
            BUF = [0, 4096]
            ABSB = 8192
            acc = svec.ap()[:, 8:8 + nchunks]
            rawtok = {}
            for j in range(nchunks):
                b0 = BUF[j % 2]
                dep = ([("scalar", p.now("scalar")), ("scdma", rawtok[j - 2])] if j >= 2 else
                       [("gpsimd", c_lib), ("vector", p.now("vector")), ("sync", p.now("sync")),
                        ("scdma", p.now("scdma"))])
                cdma = dma("gpsimd", pb(b0, 4096).rearrange("p (k c) -> p k c", k=kt),
                           csrc[:, :, j * cw:(j + 1) * cw], deps=dep)
                sdep = [("gpdma", cdma)] + ([("sync", p.now("sync"))] if j < 2 else [])
                S(lambda h, b0=b0, j=j: h.activation(pb(ABSB, 4096), pb(b0, 4096),
                                                     AF.Abs, accum_out=acc[:, j:j + 1]),
                  deps=sdep)
                rawtok[j] = dma("scalar", cdst[:, :, j * cw:(j + 1) * cw],
                                pb(b0, 4096).rearrange("p (k c) -> p k c", k=kt),
                                deps=[("gpdma", cdma)])
            cr = V(lambda h, n=nchunks: h.reduce_sum(svec.ap()[:, 20:21], acc[:, 0:n], axis=AXX),
                   deps=[("scalar", p.now("scalar"))])
            cg = G(lambda h: h.partition_all_reduce(svec.ap()[:, 21:22], svec.ap()[:, 20:21], P, RED.add),
                   deps=[("vector", cr)])
            V(lambda h: h.tensor_scalar(svec.ap()[:, 22:23], svec.ap()[:, 21:22],
                                        1.0 / (rows * cols), 1e-5, ALU.mult, ALU.max),
              deps=[("gpsimd", cg)])
            cv = V(lambda h: h.reciprocal(svec.ap()[:, svcol:svcol + 1], svec.ap()[:, 22:23]))
            for j in range(nchunks):
                b0 = BUF[j % 2]
                dep = ([("sync", p.now("sync"))] if j >= 2 else
                       [("vector", cv), ("scdma", p.now("scdma"))])
                cdma = dma("scalar", pb(b0, 4096).rearrange("p (k c) -> p k c", k=kt),
                           cdst[:, :, j * cw:(j + 1) * cw], deps=dep)
                cq = V(lambda h, b0=b0: h._custom_dve(ops["ZZWQNT"], out=pb(ABSB, 4096), in0=pb(b0, 4096),
                                                      s0=svec.ap()[:, svcol:svcol + 1], s1=MAGIC),
                       deps=[("scdma", cdma), ("sync", p.now("sync"))])
                dma("sync", cdst[:, :, j * cw:(j + 1) * cw],
                    pb(ABSB, 4096).rearrange("p (k c) -> p k c", k=kt), deps=[("vector", cq)])
            return p.now("sync")

        _PH[0] = p.idx
        if _STOP[0] <= 0:
            return finish_now()
        wprep_bit(w1T, D, 4096, sc_w1, 0)
        wprep_bit(w2T, 4096, D, sc_w2, 1)
        V(lambda h: h.reciprocal(svec.ap()[:, 2:3], svec.ap()[:, 0:1]))
        c_wbit = V(lambda h: h.reciprocal(svec.ap()[:, 3:4], svec.ap()[:, 1:2]))

        def wcast(wsrc, dst, ncols):
            csrc = wsrc.rearrange("(k p) n -> p k n", p=P)
            cdst = dst.rearrange("(k p) n -> p k n", p=P)
            stok = {}
            for j in range(ncols // 512):
                b0 = (j % 2) * 4096
                if j < 2:
                    dep = [("vector", p.now("vector")), ("sync", p.now("sync"))] + \
                          [(s, p.now(s)) for s in ("gpdma", "scdma")]
                else:
                    dep = [(stok[j - 2][0], stok[j - 2][1])]
                cld = dma("gpsimd", pb(b0, 4096).rearrange("p (k c) -> p k c", k=KT),
                          csrc[:, :, j * 512:(j + 1) * 512], deps=dep)
                sq, ss = ("scalar", "scdma") if j % 2 == 0 else ("sync", "sync")
                stok[j] = (ss, dma(sq, cdst[:, :, j * 512:(j + 1) * 512],
                                   pb(b0, 4096).rearrange("p (k c) -> p k c", k=KT), deps=[("gpdma", cld)]))
        wcast(w1qkvT, sc_qkv, 3072)
        wcast(w1oT, sc_o, 1024)
        c_castdone = [("sync", p.now("sync")), ("scdma", p.now("scdma"))]

        def fp4_qdq(src_off, n, nb, f1_off, u_off, abs_off, dst_off, S_off, deps=()):
            c1 = V(lambda h: h.reduce_max(pf(S_off, nb), pb(src_off, n).rearrange("p (b s) -> p b s", s=16),
                                          axis=AXX, apply_absolute_value=True), deps=list(deps))
            cab = S(lambda h: h.activation(pb(abs_off, n), pb(src_off, n), AF.Abs), deps=list(deps))
            V(lambda h: h.tensor_scalar(pf(S_off + 2 * nb, nb), pf(S_off, nb), 1.0 / 12, 5e-13, ALU.mult, ALU.max))
            V(lambda h: h.reciprocal(pf(S_off + 2 * nb, nb), pf(S_off + 2 * nb, nb)))
            V(lambda h: h.tensor_copy(pb(S_off + 4 * nb, nb), pf(S_off + 2 * nb, nb)))
            V(lambda h: h.tensor_scalar(pf(S_off + 2 * nb, nb), pf(S_off, nb), 1.0 / 24, 2.5e-13, ALU.mult, ALU.max))
            V(lambda h: h.tensor_copy(pb(S_off + 5 * nb, nb), pf(S_off + 2 * nb, nb)))
            r2 = pb(S_off + 4 * nb, nb).rearrange("p (b o) -> p b o", o=1).broadcast_to([P, nb, 16])
            sq = pb(S_off + 5 * nb, nb).rearrange("p (b o) -> p b o", o=1).broadcast_to([P, nb, 16])
            s3 = pb(src_off, n).rearrange("p (b s) -> p b s", s=16)
            a3 = pb(abs_off, n).rearrange("p (b s) -> p b s", s=16)
            V(lambda h: h._custom_dve(ops["ZZFP4A"], out=pb(f1_off, n).rearrange("p (b s) -> p b s", s=16),
                                      in0=s3, in1=r2, s0=4.4, s1=MAGIC))
            V(lambda h: h._custom_dve(ops["ZZFP4B"], out=pb(u_off, n).rearrange("p (b s) -> p b s", s=16),
                                      in0=a3, in1=r2, s0=5.0, s1=2.0), deps=[("scalar", cab)])
            V(lambda h: h.scalar_tensor_tensor(pb(u_off, n), pb(u_off, n), 2.0, pb(f1_off, n),
                                               ALU.add, ALU.mult))
            return V(lambda h: h.tensor_tensor(pb(dst_off, n).rearrange("p (b s) -> p b s", s=16),
                                               pb(u_off, n).rearrange("p (b s) -> p b s", s=16), sq, ALU.mult))

        def wprep_fp4b(wsrc, outdim, indim, sc_dst):
            optile = outdim // P
            ikt = indim // P
            nr = max(1, 4096 // indim)
            csrc = wsrc.rearrange("(k p) n -> p k n", p=P)
            cdst = sc_dst.rearrange("(k p) n -> p k n", p=P)
            W_, O_f1, O_u, O_tq, S_off, ABSW = 0, 4096, 8192, 12288, 16384, 18432
            r0 = 0
            while r0 < optile:
                rr = min(nr, optile - r0)
                cw = rr * indim
                nb = cw // 16
                cdma = dma("gpsimd", pb(W_, cw).rearrange("p (k c) -> p k c", k=rr),
                           csrc[:, r0:r0 + rr, :],
                           deps=[("sync", p.now("sync")), ("scdma", p.now("scdma")), ("gpsimd", c_lib),
                                 ("vector", p.now("vector")), ("tensor", p.now("tensor"))])
                cdq = fp4_qdq(W_, cw, nb, O_f1, O_u, ABSW, W_, S_off, deps=[("gpdma", cdma)])
                csw = p.now("sync")
                for bki in range(ikt):
                    for a in range(rr):
                        transpose2(pb(W_ + a * indim + bki * P, P), P, P,
                                   pb(O_tq + bki * rr * P + a * P, P),
                                   deps=[("vector", cdq), ("sync", csw)])
                cst = p.now("vector")
                for bki in range(ikt):
                    dma("sync", cdst[:, bki:bki + 1, r0 * P:(r0 + rr) * P],
                        pb(O_tq + bki * rr * P, rr * P).rearrange("p (o c) -> p o c", o=1),
                        deps=[("vector", cst)])
                r0 += rr
            return p.now("sync")

        wprep_fp4b(wgO, 2816, D, sc_wg)
        wprep_fp4b(wuO, 2816, D, sc_wu)
        wprep_fp4b(wdO, D, 2816, sc_wd)

        def do_mod(wmod, ccol, L):
            csil = S(lambda h: h.activation(svec.ap()[:, 24:32], smc(ccol, 0, 8), AF.Silu),
                     deps=[("sync", c_smalls)])
            ACC = 24576
            cz = V(lambda h: h.memset(pf(ACC, 6144)[0:1, :], 0.0),
                   deps=[("tensor", p.now("tensor")), ("sync", p.now("sync"))])
            for t in range(KT):
                off = (t % 2) * 12288
                mq, ms = ("scalar", "scdma") if t % 2 == 0 else ("sync", "sync")
                cd = dma(mq, pf(off, 6144), wmod.rearrange("(k p) n -> p k n", p=P)[:, t:t + 1, :],
                         deps=[("tensor", p.now("tensor")) if t >= 2 else ("vector", cz)])
                for nchk in range(12):
                    bnk = "A" if nchk % 2 == 0 else "B"
                    sl = (psA if nchk % 2 == 0 else psB).ap()[0:1, :]
                    cm = TT(lambda h, off=off, nchk=nchk, sl=sl, t=t: h.matmul(
                        sl, svec.ap()[:, 24 + t:25 + t], pf(off, 6144)[:, nchk * 512:(nchk + 1) * 512],
                        start=True, stop=True, skip_group_check=True),
                        deps=bdep(bnk) + [(ms, cd), ("scalar", csil)])
                    V(lambda h, nchk=nchk, sl=sl: h.tensor_add(
                        pf(ACC, 6144)[0:1, nchk * 512:(nchk + 1) * 512],
                        pf(ACC, 6144)[0:1, nchk * 512:(nchk + 1) * 512], sl),
                      deps=[("tensor", cm)])
                    breg(bnk)
            cacc = p.now("vector")
            for seg in range(48):
                strip_transpose(pf(ACC, 6144)[0:1, seg * P:(seg + 1) * P],
                                modP.ap()[:, L * 48 + seg:L * 48 + seg + 1],
                                deps=[("vector", cacc)])
            base = L * 48
            for v in (2, 5):
                cs = base + v * 8
                c1 = V(lambda h, cs=cs: h.tensor_scalar(svec.ap()[:, 40:48], modP.ap()[:, cs:cs + 8],
                                                        0.0, 2.0, ALU.is_ge, ALU.mult))
                V(lambda h: h.tensor_scalar_add(svec.ap()[:, 40:48], svec.ap()[:, 40:48], -1.0))
                c3 = S(lambda h, cs=cs: h.activation(svec.ap()[:, 48:56], modP.ap()[:, cs:cs + 8], AF.Abs),
                       deps=[("vector", c1)])
                V(lambda h: h.tensor_scalar_max(svec.ap()[:, 48:56], svec.ap()[:, 48:56], 0.25),
                  deps=[("scalar", c3)])
                V(lambda h, cs=cs: h.tensor_mul(modP.ap()[:, cs:cs + 8], svec.ap()[:, 40:48], svec.ap()[:, 48:56]))
            for v in (1, 4):
                cs = base + v * 8
                V(lambda h, cs=cs: h.tensor_scalar_add(modP.ap()[:, cs:cs + 8], modP.ap()[:, cs:cs + 8], 1.0))
            return p.now("vector")

        _PH[1] = p.idx
        if _STOP[0] <= 1:
            return finish_now()
        c_mod0 = do_mod(wmod0, "c0T", 0)

        def ln_chunk(col0, w, hoff, cvcol, svcol, sq_off, extra=()):
            V(lambda h: h.memset(svec.ap()[:, 95:96], 0.0),
              deps=list(extra) + [("tensor", p.now("tensor"))])
            for kt in range(KT):
                off = sq_off + (kt % 2) * 2 * w
                csq = S(lambda h, kt=kt, off=off: h.activation(pf(off, w), xslice(kt, col0, w), AF.Square),
                        deps=list(extra) + [("tensor", p.now("tensor")), ("vector", p.now("vector"))])
                TT(lambda h, kt=kt: h.matmul(psA.ap()[0:1, 0:w], onesf.ap()[:], xslice(kt, col0, w),
                                             start=(kt == 0), stop=(kt == KT - 1), skip_group_check=True),
                   deps=(bdep("A") + [("vector", p.now("vector"))] + list(extra)) if kt == 0 else [("vector", c_ones)])
                TT(lambda h, kt=kt, off=off: h.matmul(psB.ap()[0:1, 0:w], onesf.ap()[:], pf(off, w),
                                                      start=(kt == 0), stop=(kt == KT - 1), skip_group_check=True),
                   deps=([("scalar", csq)] + bdep("B")) if kt == 0 else [("scalar", csq)])
            cmm = p.now("tensor")
            V(lambda h: h.tensor_scalar_mul(murep.ap()[0:1, 0:w], psA.ap()[0:1, 0:w], 1.0 / D),
              deps=[("tensor", cmm)])
            breg("A")
            V(lambda h: h.tensor_scalar_mul(ivrep.ap()[0:1, 0:w], psB.ap()[0:1, 0:w], 1.0 / D))
            breg("B")
            V(lambda h: h.tensor_mul(srep.ap()[0:1, 0:w], murep.ap()[0:1, 0:w], murep.ap()[0:1, 0:w]))
            V(lambda h: h.tensor_sub(ivrep.ap()[0:1, 0:w], ivrep.ap()[0:1, 0:w], srep.ap()[0:1, 0:w]))
            cs = S(lambda h: h.activation(ivrep.ap()[0:1, 0:w], ivrep.ap()[0:1, 0:w], AF.Sqrt,
                                          bias=smc("eps")[0:1, :]), deps=[("vector", p.now("vector"))])
            cv = V(lambda h: h.reciprocal(ivrep.ap()[0:1, 0:w], ivrep.ap()[0:1, 0:w]), deps=[("scalar", cs)])
            cb1 = G(lambda h: h.partition_broadcast(murep.ap()[:, 0:w], murep.ap()[0:1, 0:w], P),
                    deps=[("vector", cv), ("gpsimd", c_lib)])
            cb2 = G(lambda h: h.partition_broadcast(ivrep.ap()[:, 0:w], ivrep.ap()[0:1, 0:w], P),
                    deps=[("vector", cv)])
            out_c = 0
            for kt in range(KT):
                off = sq_off + (kt % 2) * 2 * w
                V(lambda h, kt=kt, off=off: h.tensor_sub(pf(off, w), xslice(kt, col0, w), murep.ap()[:, 0:w]),
                  deps=[("gpsimd", cb2)])
                V(lambda h, off=off: h.tensor_mul(pf(off, w), pf(off, w), ivrep.ap()[:, 0:w]))
                out_c = V(lambda h, kt=kt, off=off: h.tensor_scalar(
                    pb(hoff + kt * w, w), pf(off, w),
                    modP.ap()[:, cvcol + kt:cvcol + kt + 1], modP.ap()[:, svcol + kt:svcol + kt + 1],
                    ALU.mult, ALU.add))
            return out_c

        WQV, W0O = 0, 16384
        H1, A0 = 24576, 28672
        QF, VF, W1Q, W1V, W2Q, W2V, SQX = 32768, 34816, 36864, 38912, 40960, 42496, 44544
        cwqv = dma("gpsimd", pb(WQV, KT * 2048).rearrange("p (k c) -> p k c", k=KT),
                   w0qvT.rearrange("(k p) n -> p k n", p=P),
                   deps=[("sync", p.now("sync")), ("vector", p.now("vector")), ("tensor", p.now("tensor"))])
        cw0o = dma("gpsimd", pb(W0O, KT * 1024).rearrange("p (k c) -> p k c", k=KT),
                   w0oT.rearrange("(k p) n -> p k n", p=P))
        for j in range(NCH):
            cc = j * 512
            chv = ln_chunk(cc, 512, H1, 0 * 48 + 1 * 8, 0 * 48 + 0 * 8, SQX,
                           extra=[("sync", c_xload), ("vector", c_mod0)])
            for pair in range(KT):
                for kt in range(KT):
                    TT(lambda h, kt=kt, pair=pair: h.matmul(
                        psA.ap()[:], pb(WQV + kt * 2048 + pair * P, P), pb(H1 + kt * 512, 512),
                        start=(kt == 0), stop=(kt == KT - 1), skip_group_check=True),
                       deps=((bdep("A") + [("vector", chv), ("gpdma", cwqv)]) if kt == 0 else []))
                cq = p.now("tensor")
                for kt in range(KT):
                    TT(lambda h, kt=kt, pair=pair: h.matmul(
                        psB.ap()[:], pb(WQV + kt * 2048 + 1024 + pair * P, P), pb(H1 + kt * 512, 512),
                        start=(kt == 0), stop=(kt == KT - 1), skip_group_check=True),
                       deps=(bdep("B") if kt == 0 else []))
                cv2 = p.now("tensor")
                cqe = V(lambda h, pair=pair: h.tensor_scalar_add(pf(QF, 512), psA.ap()[:], smc("b0qv", pair)),
                        deps=[("tensor", cq)])
                breg("A")
                cve = V(lambda h, pair=pair: h.tensor_scalar_add(pf(VF, 512), psB.ap()[:], smc("b0qv", 8 + pair)),
                        deps=[("tensor", cv2)])
                breg("B")
                for (src, W1o, W2o) in ((QF, W1Q, W2Q), (VF, W1V, W2V)):
                    V(lambda h, src=src, W1o=W1o: h.tensor_sub(
                        pf(W1o, 256), pf(src, 512)[:, 0:512:2], pf(src, 512)[:, 1:512:2]))
                    V(lambda h, src=src, W1o=W1o: h.tensor_add(
                        pf(W1o + 512, 256), pf(src, 512)[:, 0:512:2], pf(src, 512)[:, 1:512:2]))
                    V(lambda h, W1o=W1o, W2o=W2o: h.tensor_sub(
                        pf(W2o, 128), pf(W1o + 512, 256)[:, 0:256:2], pf(W1o + 512, 256)[:, 1:256:2]))
                    V(lambda h, W1o=W1o, W2o=W2o: h.tensor_add(
                        pf(W2o + 256, 128), pf(W1o + 512, 256)[:, 0:256:2], pf(W1o + 512, 256)[:, 1:256:2]))
                    V(lambda h, W2o=W2o: h.tensor_sub(
                        pf(W2o + 512, 64), pf(W2o + 256, 128)[:, 0:128:2], pf(W2o + 256, 128)[:, 1:128:2]))
                    V(lambda h, W2o=W2o: h.tensor_add(
                        pf(W2o + 640, 64), pf(W2o + 256, 128)[:, 0:128:2], pf(W2o + 256, 128)[:, 1:128:2]))
                V(lambda h, pair=pair: h.scalar_tensor_tensor(
                    pf(QF, 64), pf(W2Q + 640, 64), svec.ap()[:, 64 + pair:65 + pair], pf(W2V + 640, 64),
                    ALU.mult, ALU.mult))
                V(lambda h, pair=pair: h.scalar_tensor_tensor(
                    pf(QF + 128, 64), pf(W2Q + 512, 64), svec.ap()[:, 72 + pair:73 + pair], pf(W2V + 512, 64),
                    ALU.mult, ALU.mult))
                V(lambda h, pair=pair: h.scalar_tensor_tensor(
                    pf(QF + 256, 128), pf(W2Q, 128), svec.ap()[:, 80 + pair:81 + pair], pf(W2V, 128),
                    ALU.mult, ALU.mult))
                V(lambda h: h.scalar_tensor_tensor(
                    pf(VF, 256), pf(W1Q, 256), 2.0 ** -1.5, pf(W1V, 256), ALU.mult, ALU.mult))
                V(lambda h: h.tensor_add(pf(QF + 1024, 128)[:, 0:128:2], pf(QF, 64), pf(QF + 128, 64)))
                V(lambda h: h.tensor_sub(pf(QF + 1024, 128)[:, 1:128:2], pf(QF, 64), pf(QF + 128, 64)))
                V(lambda h: h.tensor_add(pf(W1Q, 256)[:, 0:256:2], pf(QF + 1024, 128), pf(QF + 256, 128)))
                V(lambda h: h.tensor_sub(pf(W1Q, 256)[:, 1:256:2], pf(QF + 1024, 128), pf(QF + 256, 128)))
                ab = pb(A0 + pair * 512, 512).rearrange("p (a b) -> p a b", b=2)
                V(lambda h, ab=ab: h.tensor_tensor(
                    ab[:, :, 0:1], pf(W1Q, 256).rearrange("p (a b) -> p a b", b=1),
                    pf(VF, 256).rearrange("p (a b) -> p a b", b=1), ALU.add))
                V(lambda h, ab=ab: h.tensor_tensor(
                    ab[:, :, 1:2], pf(W1Q, 256).rearrange("p (a b) -> p a b", b=1),
                    pf(VF, 256).rearrange("p (a b) -> p a b", b=1), ALU.subtract))
            c_att = p.now("vector")
            for m in range(KT):
                for kt in range(KT):
                    TT(lambda h, m=m, kt=kt: h.matmul(
                        psC.ap()[:], pb(W0O + kt * 1024 + m * P, P), pb(A0 + kt * 512, 512),
                        start=(kt == 0), stop=(kt == KT - 1), skip_group_check=True),
                       deps=((bdep("C") + [("vector", c_att), ("gpdma", cw0o)]) if kt == 0 else []))
                cmm = p.now("tensor")
                cev = V(lambda h, m=m: h.tensor_scalar(pf(SQX, 512), psC.ap()[:],
                                                 smc("b0o", m), modP.ap()[:, 2 * 8 + m:2 * 8 + m + 1],
                                                 ALU.add, ALU.mult), deps=[("tensor", cmm)], sw=False)
                breg("C")
                V(lambda h, m=m, cc=cc: h.tensor_add(xslice(m, cc, 512), xslice(m, cc, 512), pf(SQX, 512)),
                  sw=cev)

        def tap(tname):
            if tname in tapout:
                cfin = p.now("vector")
                for kt in range(KT):
                    dma("sync", tapout[tname].rearrange("(k p) n -> p k n", p=P)[:, kt:kt + 1, :],
                        xr[:, kt * T:(kt + 1) * T].rearrange("p (o n) -> p o n", o=1),
                        deps=[("vector", cfin)])
        if _STOP[0] == 15:
            cmp_ = p.now("vector")
            dma("sync", outT.rearrange("(k p) n -> p k n", p=P)[:, 0:1, 0:96],
                modP.ap()[:].rearrange("p (o n) -> p o n", o=1), deps=[("vector", cmp_)])
            p.emit("gpsimd", lambda h: h.engine_nop(), deps=[("sync", p.now("sync"))], inc=0)
            p.build(block)
            import concourse.mybir as _mb
            _mb.codegen_inst_isa_subclasses(nc)
            return nc
        _PH[2] = p.idx
        if _STOP[0] <= 2:
            return finish_now()
        tap("xattn0")

        TCH = 256
        H2, QI1, XLN = 0, 2048, 4096
        ABSA, ABSC = 6144, 7168
        WS1B = 4096
        H4A, Q2A, Q2F, SQS = 8192, 16384, 20480, 28672
        TFX = 32768
        RB1, RG2, RB2 = 34816, 38912, 43008
        WS1 = 47104
        SQX2 = 33792
        cstr = dma("sync", strip[0:1, 0:1024], strps[0:1, 0:1024],
                   deps=[("sync", p.now("sync")), ("vector", p.now("vector")), ("tensor", p.now("tensor"))])
        crep = 0
        for rj, roff in enumerate((RB1, RG2, RB2)):
            for seg in range(4):
                cs2 = V(lambda h: h.tensor_copy(pb(0, 0) if False else pool.ap()[0:1, 47104:47104 + 1024],
                                                strip[0:1, 0:1024]),
                        deps=[("sync", cstr), ("gpsimd", crep)])
                crep = G(lambda h, roff=roff, seg=seg: h.partition_broadcast(
                    pb(roff + seg * 1024, 1024), pool.ap()[0:1, 47104:47104 + 1024], P),
                    deps=[("vector", cs2)])
                if rj * 4 + seg < 11:
                    nxt = (rj * 4 + seg + 1)
                    cstr = dma("sync", strip[0:1, 0:1024], strps[0:1, nxt * 1024:(nxt + 1) * 1024],
                               deps=[("vector", p.now("vector"))])
        c_reps = crep

        wrd = {}
        for tch in range(T // TCH):
            cc = tch * TCH
            _PH[(tch, "ln")] = p.idx
            chv = ln_chunk(cc, TCH, H2, 0 * 48 + 4 * 8, 0 * 48 + 3 * 8, TFX,
                           extra=[("gpsimd", c_reps)])
            _PH[(tch, "stats")] = p.idx
            for kt in range(KT):
                off = ABSA + (kt % 2) * 512
                csq = S(lambda h, kt=kt, off=off: h.activation(pf(off, TCH), pb(H2 + kt * TCH, TCH), AF.Square),
                        deps=[("vector", chv), ("tensor", p.now("tensor"))])
                TT(lambda h, kt=kt: h.matmul(psA.ap()[0:1, 0:TCH], onesb.ap()[:], pb(H2 + kt * TCH, TCH),
                                             start=(kt == 0), stop=(kt == KT - 1), skip_group_check=True),
                   deps=((bdep("A") + [("vector", chv)]) if kt == 0 else []))
                TT(lambda h, kt=kt, off=off: h.matmul(psB.ap()[0:1, 0:TCH], onesf.ap()[:], pf(off, TCH),
                                                      start=(kt == 0), stop=(kt == KT - 1), skip_group_check=True),
                   deps=([("scalar", csq)] + bdep("B")) if kt == 0 else [("scalar", csq)])
            cmm = p.now("tensor")
            V(lambda h: h.tensor_scalar_mul(murep.ap()[0:1, 0:TCH], psA.ap()[0:1, 0:TCH], 1.0 / D),
              deps=[("tensor", cmm)])
            breg("A")
            V(lambda h: h.tensor_scalar_mul(ivrep.ap()[0:1, 0:TCH], psB.ap()[0:1, 0:TCH], 1.0 / D))
            breg("B")
            V(lambda h: h.tensor_mul(srep.ap()[0:1, 0:TCH], murep.ap()[0:1, 0:TCH], murep.ap()[0:1, 0:TCH]))
            V(lambda h: h.tensor_sub(ivrep.ap()[0:1, 0:TCH], ivrep.ap()[0:1, 0:TCH], srep.ap()[0:1, 0:TCH]))
            cs = S(lambda h: h.activation(ivrep.ap()[0:1, 0:TCH], ivrep.ap()[0:1, 0:TCH], AF.Sqrt,
                                          bias=smc("eps")[0:1, :]), deps=[("vector", p.now("vector"))])
            cv = V(lambda h: h.reciprocal(ivrep.ap()[0:1, 0:TCH], ivrep.ap()[0:1, 0:TCH]), deps=[("scalar", cs)])
            cb1 = G(lambda h: h.partition_broadcast(murep.ap()[:, 0:TCH], murep.ap()[0:1, 0:TCH], P),
                    deps=[("vector", cv)])
            cb2 = G(lambda h: h.partition_broadcast(ivrep.ap()[:, 0:TCH], ivrep.ap()[0:1, 0:TCH], P),
                    deps=[("vector", cv)])
            for kt in range(KT):
                V(lambda h, kt=kt: h.tensor_sub(pf(TFX, TCH), pb(H2 + kt * TCH, TCH), murep.ap()[:, 0:TCH]),
                  deps=[("gpsimd", cb2)])
                V(lambda h: h.tensor_mul(pf(TFX, TCH), pf(TFX, TCH), ivrep.ap()[:, 0:TCH]))
                V(lambda h, kt=kt: h.tensor_scalar(
                    pb(XLN + kt * TCH, TCH), pf(TFX, TCH), smc("ln1g", kt), smc("ln1b", kt), ALU.mult, ALU.add))
            cab0 = S(lambda h: h.activation(pf(ABSA, TCH), pb(XLN, TCH), AF.Abs),
                     deps=[("vector", p.now("vector"))])
            cacc = V(lambda h: h.tensor_copy(pf(ABSC, TCH), pf(ABSA, TCH)), deps=[("scalar", cab0)])
            for kt in range(1, KT):
                cab = S(lambda h, kt=kt: h.activation(pf(ABSA, TCH), pb(XLN + kt * TCH, TCH), AF.Abs),
                        deps=[("vector", cacc)])
                cacc = V(lambda h: h.tensor_max(pf(ABSC, TCH), pf(ABSC, TCH), pf(ABSA, TCH)),
                         deps=[("scalar", cab)])
            c_absdone = p.now("scalar")
            cgm = G(lambda h: h.partition_all_reduce(pf(ABSA, TCH), pf(ABSC, TCH), P, RED.max),
                    deps=[("vector", cacc)])
            V(lambda h: h.tensor_scalar_max(murep.ap()[0:1, 0:TCH], pf(ABSA, TCH)[0:1, :], 1e-5),
              deps=[("gpsimd", cgm)])
            V(lambda h: h.reciprocal(srep.ap()[0:1, 0:TCH], murep.ap()[0:1, 0:TCH]))
            cs1 = V(lambda h: h.tensor_scalar_mul(srep.ap()[0:1, 0:TCH], srep.ap()[0:1, 0:TCH], 127.0))
            V(lambda h: h.tensor_scalar_mul(murep.ap()[0:1, 0:TCH], murep.ap()[0:1, 0:TCH], 1.0 / 127.0))
            cr1 = V(lambda h: h.tensor_single_scalar(murep.ap()[0:1, 0:TCH], murep.ap()[0:1, 0:TCH],
                                                     svec.ap()[0:1, 2:3], ALU.mult))
            cbs = G(lambda h: h.partition_broadcast(srep.ap()[:, 0:TCH], srep.ap()[0:1, 0:TCH], P),
                    deps=[("vector", cs1)])
            _PH[(tch, "quant")] = p.idx
            cq1 = V(lambda h: h._custom_dve(
                ops["ZZACTQ"], out=pb(QI1, KT * TCH).rearrange("p (k c) -> p k c", k=KT),
                in0=pb(XLN, KT * TCH).rearrange("p (k c) -> p k c", k=KT),
                in1=srep.ap()[:, 0:TCH].rearrange("p (o n) -> p o n", o=1).broadcast_to([P, KT, TCH]),
                s0=MAGIC, s1=-128.0), deps=[("gpsimd", cbs)])
            for tt2 in range(TCH // P):
                strip_transpose(murep.ap()[0:1, tt2 * P:(tt2 + 1) * P], svec.ap()[:, 56 + tt2:57 + tt2],
                                deps=[("vector", cr1)])
            c_r1T = p.now("vector")
            _PH[(tch, "ffn1")] = p.idx
            gacc = svec.ap()[:, 8:24]
            cwds = {}
            def emit_wdma(r):
                b = WS1 if r % 2 == 0 else WS1B
                q, qs = ("sync", "sync") if r % 2 == 0 else ("scalar", "scdma")
                deps = [(qs, p.now(qs))]
                if b in wrd:
                    deps.append(("tensor", wrd[b]))
                elif tch == 0:
                    deps.append(("gpsimd", c_reps))
                if b == WS1B:
                    deps += [("vector", cq1), ("scalar", c_absdone), ("gpsimd", cgm)]
                if r < 8:
                    cwds[r] = (qs, dma(q, pb(b, 4096).rearrange("p (k c) -> p k c", k=KT),
                                       sc_w1.rearrange("(k p) n -> p k n", p=P)[:, :, r * 512:(r + 1) * 512],
                                       deps=deps))
                else:
                    cwds[r] = (qs, dma(q, pb(b, 4096).rearrange("p (k c) -> p k c", k=32),
                                       sc_w2.rearrange("(k p) n -> p k n", p=P)[:, :, (r - 8) * P:(r - 7) * P],
                                       deps=deps))
            emit_wdma(0)
            emit_wdma(1)
            for m in range(KT):
                wb = WS1 if m % 2 == 0 else WS1B
                for tt2 in range(TCH // P):
                    ps = psA if tt2 == 0 else psB
                    bnk = "A" if tt2 == 0 else "B"
                    for kt in range(KT):
                        TT(lambda h, tt2=tt2, kt=kt, ps=ps, wb=wb: h.matmul(
                            ps.ap()[:], pb(QI1 + kt * TCH + tt2 * P, P), pb(wb + kt * 512, 512),
                            start=(kt == 0), stop=(kt == KT - 1), skip_group_check=True),
                           deps=((bdep(bnk) + [("vector", cq1), cwds[m]]) if kt == 0 else []))
                    cmm = p.now("tensor")
                    cev = V(lambda h, tt2=tt2, ps=ps: h.tensor_single_scalar(
                        pf(TFX, 512), ps.ap()[:], svec.ap()[:, 56 + tt2:57 + tt2], ALU.mult),
                      deps=[("tensor", cmm), ("vector", c_r1T), ("scalar", p.now("scalar"))], sw=False)
                    breg(bnk)
                    V(lambda h, m=m: h.tensor_add(pf(TFX, 512), pf(TFX, 512), pb(RB1 + m * 512, 512)),
                      sw=cev)
                    S(lambda h, m=m, tt2=tt2: h.activation(pb(H4A + tt2 * 4096 + m * 512, 512), pf(TFX, 512),
                                                           AF.Gelu, accum_out=gacc[:, tt2 * 8 + m:tt2 * 8 + m + 1]),
                      deps=[("vector", p.now("vector"))])
                wrd[wb] = p.now("tensor")
                emit_wdma(m + 2)
            _PH[(tch, "ln2")] = p.idx
            c_gelu = p.now("scalar")
            for tt2 in range(TCH // P):
                cs2 = S(lambda h, tt2=tt2: h.activation(pb(SQS, 4096), pb(H4A + tt2 * 4096, 4096), AF.Square,
                                                        accum_out=svec.ap()[:, 40:41]),
                        deps=[("scalar", c_gelu), ("vector", p.now("vector"))])
                cvv = V(lambda h, tt2=tt2: h.reduce_sum(svec.ap()[:, 42:43], gacc[:, tt2 * 8:tt2 * 8 + 8], axis=AXX),
                        deps=[("scalar", c_gelu)])
                V(lambda h: h.tensor_scalar_mul(svec.ap()[:, 44:45], svec.ap()[:, 42:43], 1.0 / 4096),
                  deps=[("vector", cvv)])
                V(lambda h: h.tensor_scalar_mul(svec.ap()[:, 45:46], svec.ap()[:, 40:41], 1.0 / 4096),
                  deps=[("scalar", cs2)])
                V(lambda h: h.tensor_mul(svec.ap()[:, 46:47], svec.ap()[:, 44:45], svec.ap()[:, 44:45]))
                V(lambda h: h.tensor_sub(svec.ap()[:, 45:46], svec.ap()[:, 45:46], svec.ap()[:, 46:47]))
                cs3 = S(lambda h: h.activation(svec.ap()[:, 45:46], svec.ap()[:, 45:46], AF.Sqrt, bias=smc("eps")),
                        deps=[("vector", p.now("vector"))])
                V(lambda h: h.reciprocal(svec.ap()[:, 45:46], svec.ap()[:, 45:46]), deps=[("scalar", cs3)])
                V(lambda h: h.tensor_mul(svec.ap()[:, 46:47], svec.ap()[:, 44:45], svec.ap()[:, 45:46]))
                cng = V(lambda h: h.tensor_scalar_mul(svec.ap()[:, 46:47], svec.ap()[:, 46:47], -1.0))
                V(lambda h, tt2=tt2: h.tensor_scalar(pb(SQS, 4096), pb(H4A + tt2 * 4096, 4096),
                                                     svec.ap()[:, 45:46], svec.ap()[:, 46:47], ALU.mult, ALU.add),
                  deps=[("vector", cng)])
                V(lambda h: h.tensor_mul(pb(SQS, 4096), pb(SQS, 4096), pb(RG2, 4096)))
                czf = V(lambda h: h.tensor_add(pb(SQS, 4096), pb(SQS, 4096), pb(RB2, 4096)))
                V(lambda h: h.reduce_max(svec.ap()[:, 44:45], pb(SQS, 4096), axis=AXX, apply_absolute_value=True),
                  deps=[("vector", czf)])
                V(lambda h: h.tensor_scalar_max(svec.ap()[:, 44:45], svec.ap()[:, 44:45], 1e-5))
                V(lambda h: h.reciprocal(svec.ap()[:, 45:46], svec.ap()[:, 44:45]))
                cs1q = V(lambda h: h.tensor_scalar_mul(svec.ap()[:, 45:46], svec.ap()[:, 45:46], 127.0))
                V(lambda h: h.tensor_scalar_mul(svec.ap()[:, 44:45], svec.ap()[:, 44:45], 1.0 / 127.0))
                cdq = V(lambda h: h.tensor_single_scalar(svec.ap()[:, 44:45], svec.ap()[:, 44:45],
                                                         svec.ap()[:, 3:4], ALU.mult))
                cq2 = V(lambda h: h._custom_dve(ops["ZZACTQ1"], out=pb(Q2A, 4096), in0=pb(SQS, 4096),
                                                s0=svec.ap()[:, 45:46], s1=MAGIC, imm2=127.0), deps=[("vector", cs1q)])
                cq2d = V(lambda h: h.tensor_single_scalar(pb(Q2A, 4096), pb(Q2A, 4096),
                                                          svec.ap()[:, 44:45], ALU.mult), deps=[("vector", cdq)])
                for kt in range(32):
                    transpose2(pb(Q2A + kt * P, P), P, P, pb(Q2F + kt * TCH + tt2 * P, P),
                               deps=[("vector", cq2d)])
            _PH[(tch, "ffn2")] = p.idx
            c_q2f = p.now("vector")
            for m in range(KT):
                r = m + 8
                wb = WS1 if r % 2 == 0 else WS1B
                for kt in range(32):
                    TT(lambda h, m=m, kt=kt, wb=wb: h.matmul(
                        psC.ap()[:, 0:TCH], pb(wb + kt * P, P), pb(Q2F + kt * TCH, TCH),
                        start=(kt == 0), stop=(kt == 31), skip_group_check=True),
                       deps=((bdep("C") + [("vector", c_q2f), cwds[r]]) if kt == 0 else []))
                cmm = p.now("tensor")
                cev = V(lambda h, m=m: h.tensor_scalar(pf(TFX, TCH), psC.ap()[:, 0:TCH], smc("b2", m),
                                                 modP.ap()[:, 5 * 8 + m:5 * 8 + m + 1], ALU.add, ALU.mult),
                  deps=[("tensor", cmm)], sw=False)
                breg("C")
                V(lambda h, m=m, cc=cc: h.tensor_add(xslice(m, cc, TCH), xslice(m, cc, TCH), pf(TFX, TCH)),
                  sw=cev)
                wrd[wb] = p.now("tensor")
                if r + 2 < 16:
                    emit_wdma(r + 2)

        _PH[3] = p.idx
        if _STOP[0] <= 3:
            return finish_now()
        tap("xl0")
        if _TRUNC[0]:
            c_fin0 = p.now("vector")
            for kt in range(KT):
                dma("sync", outT.rearrange("(k p) n -> p k n", p=P)[:, kt:kt + 1, :],
                    xr[:, kt * T + OWN0:kt * T + OWN1].rearrange("p (o n) -> p o n", o=1),
                    deps=[("vector", c_fin0)])
            p.emit("gpsimd", lambda h: h.engine_nop(), deps=[("sync", p.now("sync"))], inc=0)
            p.build(block)
            import concourse.mybir as _mb
            _mb.codegen_inst_isa_subclasses(nc)
            return nc

        c_mod1 = do_mod(wmod1, "c1T", 1)
        KB, VT = 0, 20480
        H3C, VS, VSF = 41440, 45536, 46560
        SQX3 = 48640
        cmz = V(lambda h: h.memset(pb(VT, 20800), 0.0),
                deps=[("tensor", p.now("tensor")), ("sync", p.now("sync"))])
        c_vtz = V(lambda h: h.memset(pb(VT, 20800).rearrange("p (b o) -> p b o", o=65)[:, :, 64:65], 1.0))
        for j in range(NCH):
            cc = j * 512
            chv = ln_chunk(cc, 512, H3C, 48 + 1 * 8, 48 + 0 * 8, SQX3, extra=[("vector", c_mod1)])
            for tile in range(16):
                buf = wba if tile % 2 == 0 else wbb
                q, qs = ("gpsimd", "gpdma") if tile % 2 == 0 else ("sync", "sync")
                cwd = (qs, dma(q, buf.ap()[:].rearrange("p (k c) -> p k c", k=KT),
                          sc_qkv.rearrange("(k p) n -> p k n", p=P)[:, :, 1024 + tile * P:1024 + (tile + 1) * P],
                          deps=[("tensor", p.now("tensor")) if tile >= 2 else ("vector", chv)] +
                               (c_castdone if tile < 2 else [])))
                ps = psA if tile % 2 == 0 else psB
                bnk = "A" if tile % 2 == 0 else "B"
                for kt in range(KT):
                    TT(lambda h, kt=kt, buf=buf, ps=ps: h.matmul(
                        ps.ap()[:], buf.ap()[:, kt * P:(kt + 1) * P], pb(H3C + kt * 512, 512),
                        start=(kt == 0), stop=(kt == KT - 1), skip_group_check=True),
                       deps=((bdep(bnk) + [("vector", chv), cwd]) if kt == 0 else []))
                cmm = p.now("tensor")
                if tile < 8:
                    V(lambda h, tile=tile, cc=cc, ps=ps: h.scalar_tensor_tensor(
                        pb(KB + tile * T + cc, 512), ps.ap()[:], smc("b1qkv", 8 + tile),
                        kvm.ap()[:, cc:cc + 512], ALU.add, ALU.mult), deps=[("tensor", cmm)], sw=False)
                    breg(bnk)
                else:
                    for s2 in range(4):
                        cve = V(lambda h, tile=tile, cc=cc, ps=ps, s2=s2: h.scalar_tensor_tensor(
                            pb(VS, P), ps.ap()[:, s2 * P:(s2 + 1) * P], smc("b1qkv", 8 + tile),
                            kvm.ap()[:, cc + s2 * P:cc + (s2 + 1) * P], ALU.add, ALU.mult),
                            deps=[("tensor", p.now("tensor"))], sw=False)
                        breg(bnk)
                        for e2 in range(2):
                            B = j * 4 + s2
                            hh = (tile - 8) * 2 + e2
                            transpose2(pb(VS, P)[e2 * 64:e2 * 64 + 64, :], 64, P,
                                       pb(VT + (B * 16 + hh) * 65, 64), deps=[("vector", max(cve, c_vtz))])
        if _STOP[0] == 35:
            return finish_now()
        c_kvdone = p.now("vector")
        atrwar = {0: 0, 1: 0}
        H3B, QCB, ATR, OCB, SQW = 41440, 45536, 46048, 47584, 45536
        for qc in range(4):
            cc0 = OWN0 + qc * 512
            chv = ln_chunk(cc0, 512, H3B, 48 + 1 * 8, 48 + 0 * 8, SQW, extra=[])
            for hp in range(KT):
                buf = wba if hp % 2 == 0 else wbb
                q, qs = ("gpsimd", "gpdma") if hp % 2 == 0 else ("sync", "sync")
                cwd = (qs, dma(q, buf.ap()[:].rearrange("p (k c) -> p k c", k=KT),
                          sc_qkv.rearrange("(k p) n -> p k n", p=P)[:, :, hp * P:(hp + 1) * P],
                          deps=[("tensor", p.now("tensor")) if hp >= 2 else ("vector", chv)]))
                for kt in range(KT):
                    TT(lambda h, kt=kt, buf=buf: h.matmul(
                        psC.ap()[:], buf.ap()[:, kt * P:(kt + 1) * P], pb(H3B + kt * 512, 512),
                        start=(kt == 0), stop=(kt == KT - 1), skip_group_check=True),
                       deps=((bdep("C") + [("vector", chv), cwd]) if kt == 0 else []))
                cq = V(lambda h, hp=hp: h.tensor_scalar_add(pb(QCB, 512), psC.ap()[:], smc("b1qkv", hp)),
                       deps=[("tensor", p.now("tensor"))], sw=False)
                breg("C")
                def emit_pv(st):
                    slot, hh, Q, e2, Qt, atr, cex, par = st
                    for t5 in range(5):
                        TT(lambda h, hh=hh, t5=t5, slot=slot, Q=Q, atr=atr: h.matmul(
                            psO.ap()[:, slot * P:slot * P + P],
                            pb(VT + ((Q // P + t5) * 16 + hh) * 65, 65),
                            pb(atr + t5 * P, P), start=(t5 == 0), stop=(t5 == 4), skip_group_check=True),
                           deps=((bdep("O") + [("scalar", cex), ("vector", c_kvdone)]) if t5 == 0 else []))
                    cpv = p.now("tensor")
                    atrwar[par] = cpv
                    V(lambda h, slot=slot, e2=e2, Qt=Qt: h.tensor_copy(
                        (murep if e2 == 0 else ivrep).ap()[0:1, Qt * P:(Qt + 1) * P],
                        psO.ap()[64:65, slot * P:slot * P + P]),
                      deps=[("tensor", cpv), ("gpsimd", p.now("gpsimd"))], sw=False)
                    breg("O")
                pend = None
                for e2 in range(2):
                    hh = hp * 2 + e2
                    for Qt in range(4):
                        Q = qc * 512 + Qt * P
                        slot = e2 * 4 + Qt
                        par = slot % 2
                        if par == 0:
                            sdst = [psSC.ap()[:, t5 * P:(t5 + 1) * P] for t5 in range(5)]
                            sbanks = ("SC0", "SC1")
                            atr = ATR
                        else:
                            sdst = [psC.ap()[:, t5 * P:(t5 + 1) * P] for t5 in range(4)] + [psD.ap()[:, 0:P]]
                            sbanks = ("C", "D")
                            atr = ATR + 640
                        for t5 in range(5):
                            TT(lambda h, hh=hh, t5=t5, e2=e2, Q=Q, Qt=Qt, sdst=sdst: h.matmul(
                                sdst[t5],
                                pool.ap()[:, KB + (hh // 2) * T + Q + t5 * P: KB + (hh // 2) * T + Q + (t5 + 1) * P][e2 * 64:e2 * 64 + 64, :],
                                pb(QCB + Qt * P, P)[e2 * 64:e2 * 64 + 64, :],
                                start=True, stop=True, skip_group_check=True),
                               deps=((bdep(*sbanks) + [("vector", cq)]) if t5 == 0 else []))
                        csc = p.now("tensor")
                        if pend is not None:
                            emit_pv(pend)
                        V(lambda h, sdst=sdst: h.tensor_add(sdst[0], sdst[0], cmk.ap()[:, 0:P]),
                          deps=[("tensor", csc)], sw=False)
                        breg(sbanks[0])
                        cm4 = V(lambda h, sdst=sdst: h.tensor_add(sdst[4], sdst[4], cmk.ap()[:, P:2 * P]),
                                sw=False)
                        breg(sbanks[1])
                        if par == 0:
                            cex = S(lambda h, atr=atr: h.activation(pb(atr, 640), psSC.ap()[:, 0:640], AF.Exp,
                                                                    bias=smc("negc"), scale=0.125),
                                    deps=[("vector", cm4), ("tensor", atrwar[0])])
                        else:
                            S(lambda h, atr=atr: h.activation(pb(atr, 512), psC.ap()[:, 0:512], AF.Exp,
                                                              bias=smc("negc"), scale=0.125),
                              deps=[("vector", cm4), ("tensor", atrwar[1])])
                            cex = S(lambda h, atr=atr: h.activation(pb(atr + 512, P), psD.ap()[:, 0:P], AF.Exp,
                                                                    bias=smc("negc"), scale=0.125))
                        breg(sbanks[0], "scalar", cex)
                        breg(sbanks[1], "scalar", cex)
                        pend = (slot, hh, Q, e2, Qt, atr, cex, par)
                emit_pv(pend)
                V(lambda h: h.reciprocal(murep.ap()[0:1, :], murep.ap()[0:1, :]))
                crc = V(lambda h: h.reciprocal(ivrep.ap()[0:1, :], ivrep.ap()[0:1, :]))
                cb0 = G(lambda h: h.partition_broadcast(murep.ap()[0:64, :], murep.ap()[0:1, :], 64),
                        deps=[("vector", crc)])
                cb1 = G(lambda h: h.partition_broadcast(ivrep.ap()[0:64, :], ivrep.ap()[0:1, :], 64),
                        deps=[("vector", crc)])
                for e2 in range(2):
                    rep = murep if e2 == 0 else ivrep
                    for Qt in range(4):
                        slot = e2 * 4 + Qt
                        V(lambda h, e2=e2, Qt=Qt, slot=slot, rep=rep, hp=hp: h.tensor_tensor(
                            pool.ap()[:, OCB + hp * 512 + Qt * P: OCB + hp * 512 + (Qt + 1) * P][e2 * 64:e2 * 64 + 64, :],
                            psO.ap()[0:64, slot * P:slot * P + P], rep.ap()[0:64, Qt * P:(Qt + 1) * P], ALU.mult),
                          deps=[("gpsimd", cb1)], sw=False)
                        breg("O")
            c_oc = p.now("vector")
            for m in range(KT):
                buf = wba if m % 2 == 0 else wbb
                q, qs = ("gpsimd", "gpdma") if m % 2 == 0 else ("sync", "sync")
                cwd = (qs, dma(q, buf.ap()[:].rearrange("p (k c) -> p k c", k=KT),
                          sc_o.rearrange("(k p) n -> p k n", p=P)[:, :, m * P:(m + 1) * P],
                          deps=[("tensor", p.now("tensor")) if m >= 2 else ("vector", c_oc)]))
                for kt in range(KT):
                    TT(lambda h, m=m, kt=kt, buf=buf: h.matmul(
                        psC.ap()[:], buf.ap()[:, kt * P:(kt + 1) * P],
                        pool.ap()[:, OCB + kt * 512:OCB + (kt + 1) * 512],
                        start=(kt == 0), stop=(kt == KT - 1), skip_group_check=True),
                       deps=((bdep("C") + [("vector", c_oc), cwd]) if kt == 0 else []))
                cmm = p.now("tensor")
                cev = V(lambda h, m=m: h.tensor_scalar(pf(SQW, 512), psC.ap()[:], smc("b1o", m),
                                                 modP.ap()[:, 48 + 2 * 8 + m:48 + 2 * 8 + m + 1], ALU.add, ALU.mult),
                  deps=[("tensor", cmm)], sw=False)
                breg("C")
                V(lambda h, m=m, cc0=cc0: h.tensor_add(xslice(m, cc0, 512), xslice(m, cc0, 512), pf(SQW, 512)),
                  sw=cev)

        _PH[4] = p.idx
        if _STOP[0] <= 4:
            return finish_now()
        tap("xattn1")

        X4, XQ, XTS = 0, 4096, 8192
        H5, F1B, UB, AB5 = 12288, 23552, 26368, 29184
        HQF, SFP, TF4, WG_ = 32000, 43264, 8192, 44320
        WU_ = 48416
        wrd1 = {}
        for qc in range(4):
            cc0 = OWN0 + qc * 512
            chv = ln_chunk(cc0, 512, X4, 48 + 4 * 8, 48 + 3 * 8, TF4,
                           extra=[("tensor", p.now("tensor"))])
            c_xq = 0
            for ttile in range(4):
                cin = p.now("vector")
                for kt in range(KT):
                    transpose2(pb(X4 + kt * 512 + ttile * P, P), P, P, pb(XTS + kt * P, P),
                               deps=[("vector", chv)])
                ctr = p.now("vector")
                cdq = fp4_qdq(XTS, 1024, 64, F1B, UB, AB5, XTS + 1024, SFP, deps=[("vector", ctr), ("tensor", p.now("tensor"))])
                for kt in range(KT):
                    c_xq = transpose2(pb(XTS + 1024 + kt * P, P), P, P, pb(XQ + kt * 512 + ttile * P, P),
                                      deps=[("vector", cdq)])
            c_xqT = p.now("tensor")
            cgs = {}
            def emit_gdma(r):
                if r >= 6:
                    return
                ow = 512 if r < 5 else 256
                g, u = (WG_, WU_) if r % 2 == 0 else (X4, HQF)
                dg = [("sync", p.now("sync")), ("tensor", wrd1.get(g, p.now("tensor")))]
                if g == X4:
                    dg.append(("tensor", c_xqT))
                cga = dma("sync", pb(g, KT * ow).rearrange("p (k c) -> p k c", k=KT),
                          sc_wg.rearrange("(k p) n -> p k n", p=P)[:, :, r * 512:r * 512 + ow], deps=dg)
                cgb = dma("scalar", pb(u, KT * ow).rearrange("p (k c) -> p k c", k=KT),
                          sc_wu.rearrange("(k p) n -> p k n", p=P)[:, :, r * 512:r * 512 + ow],
                          deps=[("tensor", wrd1.get(u, p.now("tensor"))), ("scdma", p.now("scdma"))])
                cgs[r] = (cga, cgb)
            emit_gdma(0)
            emit_gdma(1)
            for och in range(6):
                ow = 512 if och < 5 else 256
                c0 = och * 512
                g, u = (WG_, WU_) if och % 2 == 0 else (X4, HQF)
                cg1, cg2 = cgs[och]
                for ttile in range(4):
                    for kt in range(KT):
                        TT(lambda h, kt=kt, ttile=ttile, ow=ow, g=g: h.matmul(
                            psA.ap()[:, 0:ow], pb(XQ + kt * 512 + ttile * P, P), pb(g + kt * ow, ow),
                            start=(kt == 0), stop=(kt == KT - 1), skip_group_check=True),
                           deps=((bdep("A") + [("vector", c_xq), ("sync", cg1)]) if kt == 0 else []))
                    for kt in range(KT):
                        TT(lambda h, kt=kt, ttile=ttile, ow=ow, u=u: h.matmul(
                            psB.ap()[:, 0:ow], pb(XQ + kt * 512 + ttile * P, P), pb(u + kt * ow, ow),
                            start=(kt == 0), stop=(kt == KT - 1), skip_group_check=True),
                           deps=((bdep("B") + [("scdma", cg2)]) if kt == 0 else []))
                    cmm = p.now("tensor")
                    csl = S(lambda h, ow=ow: h.activation(pb(XTS, ow), psA.ap()[:, 0:ow], AF.Silu),
                            deps=[("tensor", cmm), ("vector", p.now("vector"))])
                    breg("A", "scalar", csl)
                    V(lambda h, ttile=ttile, c0=c0, ow=ow: h.tensor_tensor(
                        pb(H5 + ttile * 2816 + c0, ow), pb(XTS, ow), psB.ap()[:, 0:ow], ALU.mult),
                      deps=[("scalar", csl)], sw=False)
                    breg("B")
                wrd1[g] = wrd1[u] = p.now("tensor")
                emit_gdma(och + 2)
            c_h5 = p.now("vector")
            cwds2 = {}
            def emit_ddma(r):
                if r >= 8:
                    return
                b = WG_ if r % 2 == 0 else X4
                q, qs = ("sync", "sync") if r % 2 == 0 else ("scalar", "scdma")
                cwds2[r] = (qs, dma(q, pb(b, 22 * P).rearrange("p (k c) -> p k c", k=22),
                                    sc_wd.rearrange("(k p) n -> p k n", p=P)[:, :, r * P:(r + 1) * P],
                                    deps=[(qs, p.now(qs)), ("tensor", wrd1.get(b, p.now("tensor")))]))
            emit_ddma(0)
            emit_ddma(1)
            for ttile in range(4):
                cdq = fp4_qdq(H5 + ttile * 2816, 2816, 176, F1B, UB, AB5, F1B, SFP, deps=[("vector", c_h5), ("tensor", p.now("tensor"))])
                for kt in range(22):
                    transpose2(pb(F1B + kt * P, P), P, P, pb(HQF + kt * 512 + ttile * P, P),
                               deps=[("vector", cdq)])
            c_hq = p.now("vector")
            for m in range(KT):
                b = WG_ if m % 2 == 0 else X4
                for kt in range(22):
                    TT(lambda h, m=m, kt=kt, b=b: h.matmul(
                        psC.ap()[:], pb(b + kt * P, P), pb(HQF + kt * 512, 512),
                        start=(kt == 0), stop=(kt == 21), skip_group_check=True),
                       deps=((bdep("C") + [("vector", c_hq), cwds2[m]]) if kt == 0 else []))
                cmm = p.now("tensor")
                V(lambda h, m=m, cc0=cc0: h.scalar_tensor_tensor(
                    xslice(m, cc0, 512), psC.ap()[:], modP.ap()[:, 48 + 5 * 8 + m:48 + 5 * 8 + m + 1],
                    xslice(m, cc0, 512), ALU.mult, ALU.add),
                  deps=[("tensor", cmm)])
                breg("C")
                wrd1[b] = p.now("tensor")
                emit_ddma(m + 2)

        _PH[9] = p.idx
        c_fin = p.now("vector")
        for kt in range(KT):
            dma("sync", outT.rearrange("(k p) n -> p k n", p=P)[:, kt:kt + 1, :],
                xr[:, kt * T + OWN0:kt * T + OWN1].rearrange("p (o n) -> p o n", o=1),
                deps=[("vector", c_fin)])
        c_out = p.now("sync")
        p.emit("gpsimd", lambda h: h.engine_nop(), deps=[("sync", c_out)], inc=0)

        p.build(block)
    mybir.codegen_inst_isa_subclasses(nc)
    return nc


_NC_CACHE = {}


def _host_pack(inputs):
    f32 = np.float32
    ins = {k: np.asarray(v) for k, v in inputs.items()}
    x = ins["L0_attn_query"]

    def t(a):
        return np.ascontiguousarray(a.T.astype(f32))

    Wqkv0 = ins["L0_attn_W_qkv"]
    shared = {
        "w0qvT": np.ascontiguousarray(np.concatenate([Wqkv0[0:1024], Wqkv0[2048:3072]], 0).T.astype(f32)),
        "w0oT": t(ins["L0_attn_W_out"]),
        "w1T": t(ins["L0_ffn_w1"]),
        "w2T": t(ins["L0_ffn_w2"]),
        "w1qkvT": t(ins["L1_attn_Wqkv"]),
        "w1oT": t(ins["L1_attn_Wout"]),
        "wgO": np.ascontiguousarray(ins["L1_ffn_w_gate"].astype(f32)),
        "wuO": np.ascontiguousarray(ins["L1_ffn_w_up"].astype(f32)),
        "wdO": np.ascontiguousarray(ins["L1_ffn_w_down"].astype(f32)),
        "wmod0": np.ascontiguousarray(ins["L0_wmod"].astype(f32)),
        "wmod1": np.ascontiguousarray(ins["L1_wmod"].astype(f32)),
        "identb": np.eye(P).astype(ml_dtypes.bfloat16),
        "strps": np.concatenate([ins["L0_ffn_b1"], ins["L0_ffn_ln2_g"], ins["L0_ffn_ln2_b"]]).astype(f32)[None, :],
    }
    cm = np.zeros((P, 256), f32)
    ii = np.arange(P)[:, None]
    rr = np.arange(P)[None, :]
    cm[:, 0:P] = np.where(ii >= rr, 0.0, -1e9)
    cm[:, P:2 * P] = np.where(ii < rr, 0.0, -1e9)
    shared["cmask"] = cm

    def colP(v):
        return np.ascontiguousarray(np.asarray(v, f32).reshape(-1, P).T)

    SM = {}
    _c = [0]
    for nm, n in [("c0T", 8), ("c1T", 8), ("b0qv", 16), ("b0o", 8), ("wqP", 24), ("wvP", 24),
                  ("ln1g", 8), ("ln1b", 8), ("b2", 8), ("b1qkv", 24), ("b1o", 8),
                  ("negc", 1), ("eps", 1)]:
        SM[nm] = _c[0]; _c[0] += n

    cores = []
    for c in range(8):
        b, half = c // 2, c % 2
        xTv = np.zeros((D, T), f32)
        kv = np.zeros((P, T), ml_dtypes.bfloat16)
        if half == 0:
            xTv[:, 256:T] = x[b, 0:2304].T
            kv[:, 256:T] = 1.0
        else:
            xTv[:, 0:2304] = x[b, 1792:4096].T
            kv[:, 0:2304] = 1.0
        smal = np.zeros((P, 160), f32)
        smal[:, SM["c0T"]:SM["c0T"] + 8] = colP(ins["L0_c"][b])
        smal[:, SM["c1T"]:SM["c1T"] + 8] = colP(ins["L1_c"][b])
        bqv = np.concatenate([ins["L0_attn_b_qkv"][0:1024], ins["L0_attn_b_qkv"][2048:3072]])
        smal[:, SM["b0qv"]:SM["b0qv"] + 16] = colP(bqv)
        smal[:, SM["b0o"]:SM["b0o"] + 8] = colP(ins["L0_attn_b_out"])
        for j in range(3):
            smal[:, SM["wqP"] + 8 * j:SM["wqP"] + 8 * (j + 1)] = colP(ins["L0_attn_weight_q"][:, j, :].reshape(-1))
            smal[:, SM["wvP"] + 8 * j:SM["wvP"] + 8 * (j + 1)] = colP(ins["L0_attn_weight_v"][:, j, :].reshape(-1))
        smal[:, SM["ln1g"]:SM["ln1g"] + 8] = colP(ins["L0_ffn_ln1_g"])
        smal[:, SM["ln1b"]:SM["ln1b"] + 8] = colP(ins["L0_ffn_ln1_b"])
        smal[:, SM["b2"]:SM["b2"] + 8] = colP(ins["L0_ffn_b2"])
        smal[:, SM["b1qkv"]:SM["b1qkv"] + 24] = colP(ins["L1_attn_bqkv"])
        smal[:, SM["b1o"]:SM["b1o"] + 8] = colP(ins["L1_attn_bout"])
        smal[:, SM["negc"]] = -CSHIFT
        smal[:, SM["eps"]] = LNEPS
        d = {"xT": xTv, "kvmask": kv, "smal": smal}
        d.update(shared)
        cores.append(d)
    return cores


def kernel(**inputs):
    taps = tuple(_TAPS)
    if taps not in _NC_CACHE:
        _NC_CACHE[taps] = build_nc(taps)
    nc = _NC_CACHE[taps]
    in_maps = _host_pack(inputs)
    res = run_bass_kernel_spmd(nc, in_maps, core_ids=list(range(8)))
    out = np.zeros((4, 4096, D), np.float32)
    for c in range(8):
        b, half = c // 2, c % 2
        out[b, half * 2048:(half + 1) * 2048, :] = np.asarray(res.results[c]["outT"]).T
    if taps:
        kernel._tapres = [{t: np.asarray(res.results[c]["tap_" + t]) for t in taps} for c in range(8)]
    return out
